# version 08 (final)
# baseline (speedup 1.0000x reference)
import contextlib
import numpy as np

import concourse.bass as bass
import concourse.mybir as mybir
from concourse.bass_utils import run_bass_kernel_spmd

F32 = mybir.dt.float32
BF16 = mybir.dt.bfloat16
AF = mybir.ActivationFunctionType
ALU = mybir.AluOpType
AX = mybir.AxisListType

P = 128
D = 1024
H = 4
DK = 256
NKC = 8
SLAB = 512
SUPER = 1024
FIR_K = (3, 15, 63)
FIR_OFF = (0, 3, 18)
NTAP_FIR = 81
QUAKE = 0x5F3759DF


class Eng:
    def __init__(self, name, sem):
        self.name = name
        self.sem = sem
        self.n = 0
        self.waited = {}
        self.script = []

    def emit(self, fn):
        self.script.append(fn)

    def mile_next(self, k=1):
        self.n += k
        return self.n


class Buf:
    __slots__ = ("ready", "readers", "name")

    def __init__(self, name="?"):
        self.name = name
        self.ready = {}
        self.readers = {}


class Ctx:

    def __init__(self, nc, sems):
        self.nc = nc
        self.e = {n: Eng(n, sems[n]) for n in ("pe", "dve", "act", "pool", "sp",
                                               "ld0", "ld1", "dg0", "dg1", "dg2", "dg3", "st")}
        self.log = []

    def wait(self, dst, srcname, tok):
        if tok <= 0:
            return
        eng = self.e[dst]
        if srcname == dst:
            return
        if eng.waited.get(srcname, 0) >= tok:
            return
        eng.waited[srcname] = tok
        self.log.append(f"{dst}: WAIT {srcname} >= {tok}")
        sem = self.e[srcname].sem
        eng.emit(lambda h, s=sem, t=tok: h.wait_ge(s, t))

    def op(self, engname, fn, reads=(), writes=(), inc=1, sem_eng=None):
        eng = self.e[engname]
        seng = self.e[sem_eng] if sem_eng else eng
        for b in reads:
            for en, t in b.ready.items():
                self.wait(engname, en, t)
        for b in writes:
            for en, t in b.ready.items():
                self.wait(engname, en, t)
            for en, t in b.readers.items():
                self.wait(engname, en, t)
        tok = seng.mile_next(inc)
        sname = seng.name
        self.log.append(f"{engname}: OP {sname}={tok}")
        sem = seng.sem

        def run(h, fn=fn, sem=sem, inc=inc):
            fn(h).then_inc(sem, inc)

        eng.emit(run)
        for b in reads:
            if b.readers.get(sname, 0) < tok:
                b.readers[sname] = tok
        for b in writes:
            if b.ready.get(sname, 0) < tok:
                b.ready[sname] = tok
            b.readers = {}
        return (sname, tok)


DEBUG_TAPS = []
PHASES = "PDFWN"


def build_core(L=8192, NCORES=8):
    nsuper = L // SUPER
    nc = bass.Bass("TRN2", target_bir_lowering=False)
    dbg_slots = {}
    if DEBUG_TAPS:
        dbg_b = nc.dram_tensor("dbg_b", [len(DEBUG_TAPS), P, 9 * SUPER], BF16,
                               kind="ExternalOutput").ap()
        dbg_f = nc.dram_tensor("dbg_f", [len(DEBUG_TAPS), P, 9 * SUPER], F32,
                               kind="ExternalOutput").ap()

    x_hbm = nc.dram_tensor("x", [L, D], F32, kind="ExternalInput").ap()
    wqT = nc.dram_tensor("wqT", [D, D], F32, kind="ExternalInput").ap()
    wkT = nc.dram_tensor("wkT", [D, D], F32, kind="ExternalInput").ap()
    wvT = nc.dram_tensor("wvT", [D, D], F32, kind="ExternalInput").ap()
    woT = nc.dram_tensor("woT", [D, D], F32, kind="ExternalInput").ap()
    gbT = nc.dram_tensor("gbT", [D, 24], F32, kind="ExternalInput").ap()
    gb_bias = nc.dram_tensor("gb_bias", [24, 1], F32, kind="ExternalInput").ap()
    convw = nc.dram_tensor("convw", [D, 12], F32, kind="ExternalInput").ap()
    firw = nc.dram_tensor("firw", [D, NTAP_FIR], F32, kind="ExternalInput").ap()
    win_h = nc.dram_tensor("win", [D, 64], F32, kind="ExternalInput").ap()
    woutE = nc.dram_tensor("woutE", [65, D], F32, kind="ExternalInput").ap()
    smalls = nc.dram_tensor("smalls", [64, 50], F32, kind="ExternalInput").ap()
    nw_h = nc.dram_tensor("nw", [P, 2], F32, kind="ExternalInput").ap()
    hrow = nc.dram_tensor("hrow", [1, 8], F32, kind="ExternalInput").ap()
    y_hbm = nc.dram_tensor("y", [L, D], F32, kind="ExternalOutput").ap()
    NDIAG = 96 + 8 * NTAP_FIR
    diag_dram = nc.dram_tensor("diag_scratch", [NDIAG, P, P], BF16, kind="Internal").ap()

    es = contextlib.ExitStack()
    E = es.enter_context
    with es:
        wq_b = E(nc.sbuf_tensor("wq_b", [P, NKC, D], BF16))
        wk_b = E(nc.sbuf_tensor("wk_b", [P, NKC, D], BF16))
        wv_b = E(nc.sbuf_tensor("wv_b", [P, NKC, D], BF16))
        wo_b = E(nc.sbuf_tensor("wo_b", [P, NKC, D], BF16))
        gb_b = E(nc.sbuf_tensor("gb_b", [P, NKC, 24], BF16))
        gbb_b = E(nc.sbuf_tensor("gbb_b", [24, 1], F32))
        win_b = E(nc.sbuf_tensor("win_b", [P, NKC, 64], BF16))
        woutE_b = E(nc.sbuf_tensor("woutE_b", [P, D], BF16))

        small_f = E(nc.sbuf_tensor("small_f", [64, 50], F32))
        sc_f = E(nc.sbuf_tensor("sc_f", [P, 12], F32))
        masks = E(nc.sbuf_tensor("masks", [P, 4, P], BF16))
        rfin_s = E(nc.sbuf_tensor("rfin_s", [P, H, P], BF16))
        ones_col = E(nc.sbuf_tensor("ones_col", [1, P], F32))
        pidx_f = E(nc.sbuf_tensor("pidx_f", [P, 1], F32))
        S_f = E(nc.sbuf_tensor("S_f", [P, 2, H, DK], F32))
        S_b = E(nc.sbuf_tensor("S_b", [P, 2, H, DK], BF16))
        vwin = E(nc.sbuf_tensor("vwin", [P, NKC, 64 + SUPER], BF16))
        o_tm = E(nc.sbuf_tensor("o_tm", [P, 8, D], BF16))
        prob_t = E(nc.sbuf_tensor("prob_t", [P, 8, 24], F32))
        stat_t = E(nc.sbuf_tensor("stat_t", [P, 8, 24], F32))
        y1n_t = E(nc.sbuf_tensor("y1n_t", [P, 4, D], BF16))
        x_f32 = [E(nc.sbuf_tensor("x_f32_0", [P, D], F32))] * 2
        x_fm = E(nc.sbuf_tensor("x_fm", [P, NKC, SLAB], BF16))
        fm_scr = x_fm
        pre_b = E(nc.sbuf_tensor("pre_b", [P, NKC, 4 + SLAB], BF16))
        x_tmb = pre_b[:].rearrange("p a b -> p (a b)")[:, 0:4 * D].rearrange(
            "p (a b) -> p a b", a=4)
        hist_b = E(nc.sbuf_tensor("hist_b", [P, 3, NKC, 4], BF16))
        qk_sb = E(nc.sbuf_tensor("qk_sb", [P, 2, NKC, SLAB], BF16))
        q_sb = qk_sb[:, 0]
        k_sb = qk_sb[:, 1]
        _qk_f32 = qk_sb[:].rearrange("p a b c -> p (a b c)").bitcast(F32)
        fw_f = _qk_f32[:, 0:NKC * NTAP_FIR].rearrange("p (a b) -> p a b", a=NKC)
        cw_f = _qk_f32[:, 704:704 + NKC * 12].rearrange("p (a b) -> p a b", a=NKC)
        iota_f = _qk_f32[:, 850:850 + P]
        hrow_f = _qk_f32[0:1, 990:1006]
        dstr = [E(nc.sbuf_tensor(f"dstr{i}", [P, 8, P], BF16)) for i in range(2)]
        fb = E(nc.sbuf_tensor("fb", [P, 3, 2, SUPER], BF16))
        gsc_f = E(nc.sbuf_tensor("gsc_f", [P, 4, 8], F32))
        kpos_s2 = [E(nc.sbuf_tensor(f"kpos_s{i}", [P, D], BF16)) for i in range(2)]
        kbn_s2 = [E(nc.sbuf_tensor(f"kbn_s{i}", [P, D], BF16)) for i in range(2)]
        vb_s2 = [E(nc.sbuf_tensor(f"vb_s{i}", [P, D], BF16)) for i in range(2)]
        uadj_s = E(nc.sbuf_tensor("uadj_s", [P, H, DK], BF16))
        a_s = E(nc.sbuf_tensor("a_s", [P, H, P], BF16))
        at_s = E(nc.sbuf_tensor("at_s", [P, H, P], BF16))
        zz_s = E(nc.sbuf_tensor("zz_s", [P, H, P], BF16))
        ww_s = zz_s
        bt_s = E(nc.sbuf_tensor("bt_s", [P, H, P], BF16))
        pq_s = [E(nc.sbuf_tensor(f"pq_s{i}", [P, H, P], BF16)) for i in range(4)]
        rr_s = [E(nc.sbuf_tensor(f"rr_s{i}", [P, H, P], BF16)) for i in range(4)]
        wn_s = E(nc.sbuf_tensor("wn_s", [P, 2, P], BF16))
        acc_f = y1n_t[:].rearrange("p a b -> p (a b)").bitcast(F32).rearrange(
            "p (a b) -> p a b", a=8)
        sq_scr = E(nc.sbuf_tensor("sq_scr", [P, SLAB], F32))
        gl_b = sq_scr[:].bitcast(BF16)[0:24, 0:SLAB]
        rms_f = E(nc.sbuf_tensor("rms_f", [P, 8, H], F32))
        ln_f = E(nc.sbuf_tensor("ln_f", [P, 16], F32))
        _pre_f32 = pre_b[:].rearrange("p a b -> p (a b)").bitcast(F32)
        _pre_b16 = pre_b[:].rearrange("p a b -> p (a b)")
        ffs_f = _pre_f32[0:64, 0:SLAB]
        ffh_f = _pre_f32[0:64, SLAB:2 * SLAB].unsqueeze(1)
        ffg_f = _pre_f32[0:64, 2 * SLAB:3 * SLAB]
        ffg_b = _pre_b16[:, 3 * SLAB * 2:3 * SLAB * 2 + SLAB]
        _otm_flat = o_tm[:].rearrange("p a b -> p (a b)")
        y2_f = [_otm_flat[:, 2048 * i:2048 * (i + 1)].bitcast(F32) for i in range(2)]
        y1_f = y2_f[0]
        tiny_f = E(nc.sbuf_tensor("tiny_f", [P, 16], F32))

        ps_proj = [E(nc.psum_tensor(f"ps_proj{i}", [P, SLAB], F32)) for i in range(2)]
        ps_conv = E(nc.psum_tensor("ps_conv", [P, SLAB], F32))
        ps_fir = E(nc.psum_tensor("ps_fir", [P, SUPER], F32))
        ps_g = E(nc.psum_tensor("ps_g", [P, H, P], F32))
        ps_u = E(nc.psum_tensor("ps_u", [P, SLAB], F32))
        ps_tr = E(nc.psum_tensor("ps_tr", [P, NKC, P], BF16))
        ps_tr2 = ps_g[:].rearrange("p a b -> p (a b)").bitcast(BF16).rearrange(
            "p (a b) -> p a b", a=NKC)
        ps_kv = ps_fir[:].bitcast(BF16).rearrange("p (a b) -> p a b", a=16)
        ps_tr32 = ps_tr[:].rearrange("p a b -> p (a b)").bitcast(F32).rearrange(
            "p (a b) -> p a b", a=H)

        sems = {n: E(nc.semaphore(n)) for n in ("pe", "dve", "act", "pool", "sp",
                                                "ld0", "ld1", "dg0", "dg1", "dg2", "dg3", "st")}
        blk_g = E(nc.Block())

        C = Ctx(nc, sems)
        B = {}

        def buf(name):
            if name not in B:
                B[name] = Buf(name)
            return B[name]

        def dma(qeng, out, in_, reads=(), writes=(), sem_eng=None):
            return C.op(qeng, lambda h, o=out, i=in_: h.dma_start(out=o, in_=i),
                        reads=reads, writes=writes, inc=16, sem_eng=sem_eng)

        def quake_rsqrt(dst, src, eps, scr_x, scr_t, bufs, nr=2):
            rd = list(bufs) + [buf("quake_scr")]
            yi = dst.bitcast(mybir.dt.int32)
            xi = scr_x.bitcast(mybir.dt.int32)
            ti = scr_t.bitcast(mybir.dt.int32)
            C.op("pool", lambda h: h.tensor_scalar(out=scr_x, in0=src, scalar1=float(eps),
                                                   scalar2=None, op0=ALU.add), reads=rd, writes=rd)
            C.op("dve", lambda h: h.tensor_scalar(out=ti, in0=xi, scalar1=1, scalar2=None,
                                                  op0=ALU.arith_shift_right), reads=rd, writes=rd)
            C.op("pool", lambda h: h.tensor_copy(yi, ti), reads=rd, writes=rd)
            C.op("dve", lambda h: h.tensor_scalar(out=yi, in0=yi, scalar1=QUAKE + 1,
                                                  scalar2=-1, op0=ALU.subtract, op1=ALU.mult),
                 reads=rd, writes=rd)
            for _ in range(nr):
                C.op("pool", lambda h: h.tensor_tensor(out=scr_t, in0=dst, in1=dst, op=ALU.mult),
                     reads=rd, writes=rd)
                C.op("dve", lambda h: h.tensor_tensor(out=scr_t, in0=scr_t, in1=scr_x,
                                                      op=ALU.mult), reads=rd, writes=rd)
                C.op("pool", lambda h: h.tensor_scalar(out=scr_t, in0=scr_t, scalar1=-0.5,
                                                      scalar2=1.5, op0=ALU.mult, op1=ALU.add),
                     reads=rd, writes=rd)
                C.op("dve", lambda h: h.tensor_tensor(out=dst, in0=dst, in1=scr_t, op=ALU.mult),
                     reads=rd, writes=rd)

        C.op("pool", lambda h: h.memset(masks[:], 1.0), writes=[buf("masks")])
        mask_defs = ((0, ALU.is_equal, -1, 1),
                     (1, ALU.is_gt, -1, 1),
                     (2, ALU.is_gt, 1, -1),
                     (3, ALU.is_ge, 1, -1))
        for mi, cmp, coef, cm in mask_defs:
            C.op("pool", lambda h, mi=mi, cmp=cmp, coef=coef, cm=cm: h.affine_select(
                out=masks[:, mi, :], in_=masks[:, mi, :], compare_op=cmp, fill=0.0,
                base=0, pattern=[[coef, P]], channel_multiplier=cm), writes=[buf("masks")])
        C.op("pool", lambda h: h.memset(ones_col[:], 1.0), writes=[buf("ones")])
        C.op("pool", lambda h: h.iota(iota_f, pattern=[[1, P]], base=0, channel_multiplier=0,
                                      allow_small_or_imprecise_dtypes=True),
             writes=[buf("iota"), buf("qk_sb")])
        C.op("pool", lambda h: h.iota(pidx_f[:], pattern=[[0, 1]], base=0, channel_multiplier=1,
                                      allow_small_or_imprecise_dtypes=True), writes=[buf("iota")])
        C.op("pool", lambda h: h.memset(S_f[:], 0.0),
             writes=[buf("S_f%d" % _h) for _h in range(H)])
        C.op("pool", lambda h: h.memset(S_b[:], 0.0),
             writes=[buf("S_b%d" % _h) for _h in range(H)])
        C.op("pool", lambda h: h.memset(vwin[:, :, 0:64], 0.0), writes=[buf("vwin")])
        C.op("pool", lambda h: h.memset(hist_b[:], 0.0), writes=[buf("hist")])

        dma("sp", cw_f[:], convw.rearrange("(c p) j -> p c j", p=P), writes=[buf("qk_sb")])
        dma("sp", fw_f[:], firw.rearrange("(c p) j -> p c j", p=P), writes=[buf("qk_sb")])
        dma("sp", small_f[:], smalls, writes=[buf("small")])
        dma("sp", hrow_f[0:1, 0:8], hrow, writes=[buf("hrow"), buf("qk_sb")])
        dma("sp", tiny_f[:, 0:2], nw_h[:, 0:2], writes=[buf("nw")])
        bXS = [buf("xs0"), buf("xs0")]
        dma("sp", x_f32[0][:, 0:192].rearrange("p (c j) -> p c j", c=8),
            gbT.rearrange("(c p) j -> p c j", p=P), writes=[bXS[0]])
        C.op("dve", lambda h: h.tensor_copy(gb_b[:], x_f32[0][:, 0:192].rearrange(
            "p (c j) -> p c j", c=8)), reads=[bXS[0]], writes=[buf("gb_b")])
        dma("sp", gbb_b[:], gb_bias, writes=[buf("gbb")])
        dma("sp", x_f32[0][:, 0:512].rearrange("p (c j) -> p c j", c=8),
            win_h.rearrange("(c p) j -> p c j", p=P), writes=[bXS[0]])
        C.op("dve", lambda h: h.tensor_copy(win_b[:], x_f32[0][:, 0:512].rearrange(
            "p (c j) -> p c j", c=8)), reads=[bXS[0]], writes=[buf("win_b")])
        dma("sp", x_f32[1][0:65, :], woutE, writes=[bXS[1]])
        C.op("dve", lambda h: h.tensor_copy(woutE_b[0:65, :], x_f32[1][0:65, :]),
             reads=[bXS[1]], writes=[buf("woutE")])

        for (w_h, w_t, bname) in ((wqT, wq_b, "wq"), (wkT, wk_b, "wk"), (wvT, wv_b, "wv")):
            for kc in range(NKC):
                rr = w_h.rearrange("(c p) i -> p c i", p=P)[:, kc, :]
                dma("sp", x_f32[kc % 2][:], rr, writes=[bXS[kc % 2]],
                    sem_eng="ld0")
                C.op("dve", lambda h, w_t=w_t, kc=kc: h.tensor_copy(
                    w_t[:, kc, :], x_f32[kc % 2][:]), reads=[bXS[kc % 2]],
                    writes=[buf(bname)])
        for kc in range(NKC):
            rr = woT.rearrange("(c p) i -> p c i", p=P)[:, kc, :]
            dma("sp", x_f32[kc % 2][:], rr, writes=[bXS[kc % 2]],
                sem_eng="ld0")
            C.op("dve", lambda h, kc=kc: h.tensor_scalar(
                out=wo_b[:, kc, :], in0=x_f32[kc % 2][:], scalar1=tiny_f[:, kc % 2:kc % 2 + 1],
                scalar2=None, op0=ALU.mult), reads=[bXS[kc % 2], buf("nw")], writes=[buf("wo")])

        bH = buf("hrow")
        C.op("act", lambda h: h.activation(hrow_f[0:1, 8:12], hrow_f[0:1, 0:4], AF.Exp,
                                           scale=-1.0), reads=[bH], writes=[bH])
        C.op("dve", lambda h: h.tensor_scalar(out=hrow_f[0:1, 12:16], in0=hrow_f[0:1, 4:8],
                                              scalar1=0.0, scalar2=0.2, op0=ALU.max,
                                              op1=ALU.min), reads=[bH], writes=[bH])
        C.op("pe", lambda h: h.matmul(ps_conv[:, 0:4], ones_col[:], hrow_f[0:1, 8:12],
                                      start=True, stop=True), reads=[bH, buf("ones")],
             writes=[buf("ps_conv")])
        C.op("pe", lambda h: h.matmul(ps_conv[:, 8:12], ones_col[:], hrow_f[0:1, 12:16],
                                      start=True, stop=True), reads=[bH], writes=[buf("ps_conv")])
        C.op("dve", lambda h: h.tensor_copy(sc_f[:, 0:4], ps_conv[:, 0:4]),
             reads=[buf("ps_conv")], writes=[buf("sc")])
        C.op("dve", lambda h: h.tensor_copy(sc_f[:, 8:12], ps_conv[:, 8:12]),
             reads=[buf("ps_conv")], writes=[buf("sc")])
        C.op("pool", lambda h: h.tensor_scalar(out=sc_f[:, 4:8], in0=sc_f[:, 8:12],
                                               scalar1=-5.0, scalar2=1.0, op0=ALU.mult,
                                               op1=ALU.add),
             reads=[buf("sc")], writes=[buf("sc")])

        dg_big = o_tm[:].rearrange("p a b -> p (a b)")[:, 0:2 * 8 * P].rearrange(
            "p (s a b) -> p s a b", s=2, a=8)
        bDGB = [buf("dgb0"), buf("dgb1")]
        _coefs = []
        for pj in range(3):
            for kc in range(NKC):
                for j in range(4):
                    _coefs.append((cw_f[:, kc, 4 * pj + j:4 * pj + j + 1],
                                   (pj * 8 + kc) * 4 + j))
        for kc in range(NKC):
            for t in range(NTAP_FIR):
                _coefs.append((fw_f[:, kc, t:t + 1], 96 + kc * NTAP_FIR + t))
        gi = 0
        while gi < len(_coefs):
            n = 1
            while (n < 8 and gi + n < len(_coefs)
                   and _coefs[gi + n][1] == _coefs[gi][1] + n):
                n += 1
            sslot = (gi // 8) % 2
            for k2 in range(n):
                C.op("dve", lambda h, sslot=sslot, k2=k2, c=_coefs[gi + k2][0]:
                     h.tensor_scalar(
                    out=dg_big[:, sslot, k2, :], in0=iota_f, scalar1=pidx_f[:], scalar2=c,
                    op0=ALU.is_equal, op1=ALU.mult), reads=[buf("iota"), buf("qk_sb")],
                    writes=[bDGB[sslot]])
            dma("pool", diag_dram[_coefs[gi][1]:_coefs[gi][1] + n, :, :].rearrange(
                "g p c -> p g c"), dg_big[:, sslot, 0:n, :],
                reads=[bDGB[sslot], buf("o_tm")])
            gi += n

        bXF = [buf("xs0"), buf("xs0")]
        bXT = buf("pre")
        bXFM = buf("x_fm")
        bPRE = buf("pre")
        bHIST = buf("hist")
        bQ = buf("qk_sb")
        bK = buf("qk_sb")
        bVW = buf("vwin")
        bPROB = buf("prob")
        bSTAT = buf("stat")
        bOTM = buf("o_tm")
        bY1N = buf("y1n")
        bFB = buf("qk_sb")
        bACC = buf("y1n")
        bRMS = buf("rms")
        bDS = [buf(f"dstr{i}") for i in range(2)]
        bGL = buf("sq")
        bGSC = buf("gsc")
        bPS_PROJ = [buf("ps_proj0"), buf("ps_proj1")]
        bPS_CONV = buf("ps_conv")
        bPS_FIR = buf("ps_fir")
        bPS_G = buf("ps_g")
        bPS_U0 = buf("ps_u0")
        bPS_U1 = buf("ps_u1")
        bPS_TR = buf("ps_tr")
        bSF4 = [buf("S_f%d" % _h) for _h in range(H)]
        bSB4 = [buf("S_b%d" % _h) for _h in range(H)]
        bKPOS2 = [buf("kpos0"), buf("kpos1")]
        bKBN2 = [buf("kbn0"), buf("kbn1")]
        bVB2 = [buf("vb0"), buf("vb1")]
        bUADJ = buf("uadj")
        bA = buf("a_s")
        bAT = buf("at_s")
        bZZ = buf("zz")
        bWW = buf("zz")
        bBT = buf("bt")
        bPQ = [buf(f"pq{i}") for i in range(4)]
        bRR = [buf(f"rr{i}") for i in range(4)]
        bWN = buf("wn")
        bFM = buf("x_fm")
        bY1F = buf("o_tm")
        bLN = buf("ln")
        bSQ = buf("sq")
        bFFS = buf("pre")
        bFFH = buf("pre")
        bFFG = buf("pre")
        bFFGB = buf("pre")
        bY2 = [buf("o_tm"), buf("o_tm")]
        bSTA = buf("stat_any")

        wbufs = {"wq": buf("wq"), "wk": buf("wk"), "wv": buf("wv"), "wo": buf("wo")}
        _trrot = [0]

        def tr_slot():
            _trrot[0] ^= 1
            return (ps_tr, bPS_TR) if _trrot[0] else (ps_tr2, bPS_G)
        dsi = 0

        def stream_diags(idx0, n):
            nonlocal dsi
            s = dsi % 2
            dsi += 1
            dma("pool", dstr[s][:, 0:n, :],
                diag_dram[idx0:idx0 + n, :, :].rearrange("g p c -> p g c"),
                writes=[bDS[s]], sem_eng=f"dg{s}")
            return s

        def dump(name, ap, rbufs):
            if name not in DEBUG_TAPS:
                return
            i = DEBUG_TAPS.index(name)
            fs = ap.shape[1]
            tgt = dbg_b if ap.dtype == BF16 else dbg_f
            npart = ap.shape[0]
            dma("pool", tgt[i, 0:npart, 0:fs], ap, reads=list(rbufs))

        def phase_P(u, s):
            gslab = u * 2 + s
            t0 = gslab * SLAB
            for blk in range(4):
                tb = t0 + blk * P
                xb = bXF[blk % 2]
                dma("sp", x_f32[blk % 2][:], x_hbm[tb:tb + P, :], writes=[xb],
                    sem_eng="ld0")
                C.op("dve", lambda h, blk=blk: h.tensor_copy(x_tmb[:, blk, :],
                                                             x_f32[blk % 2][:]),
                     reads=[xb], writes=[bXT])
            for blk in range(4):
                pt, bpt = tr_slot()
                for kc in range(NKC):
                    C.op("pe", lambda h, blk=blk, kc=kc, pt=pt: h.transpose(
                        pt[:, kc, :], x_tmb[:, blk, kc * P:kc * P + P], masks[:, 0, :]),
                        reads=[bXT, buf("masks")], writes=[bpt])
                C.op("act", lambda h, blk=blk, pt=pt: h.copy(
                    x_fm[:, :, blk * P:blk * P + P], pt[:]), reads=[bpt], writes=[bXFM])
            for kc in range(NKC):
                C.op("pe", lambda h, kc=kc: h.matmul(
                    ps_conv[0:24, :], gb_b[:, kc, :], x_fm[:, kc, :],
                    start=(kc == 0), stop=(kc == NKC - 1)),
                    reads=[buf("gb_b"), bXFM], writes=[bPS_CONV])
            C.op("dve", lambda h: h.tensor_scalar(out=gl_b[:], in0=ps_conv[0:24, :],
                                                  scalar1=gbb_b[:], scalar2=None,
                                                  op0=ALU.add),
                 reads=[bPS_CONV, buf("gbb")], writes=[bGL])
            for blk in range(4):
                C.op("pe", lambda h, blk=blk: h.transpose(
                    ps_tr[:, blk, 0:24], gl_b[0:24, blk * P:blk * P + P], masks[0:24, 0, 0:24]),
                    reads=[bGL, buf("masks")], writes=[bPS_TR])
            C.op("act", lambda h, s=s: h.copy(prob_t[:, 4 * s:4 * s + 4, :],
                                              ps_tr[:, 0:4, 0:24]),
                 reads=[bPS_TR], writes=[bPROB])
            sb0 = 4 * s
            pv = prob_t[:, sb0:sb0 + 4, :]
            C.op("act", lambda h: h.activation(pv[:, :, 20:24], pv[:, :, 20:24], AF.Sigmoid),
                 reads=[bPROB], writes=[bPROB])
            for hh in range(H):
                C.op("dve", lambda h, hh=hh: h.reduce_max(gsc_f[:, :, hh:hh + 1],
                                                          pv[:, :, 5 * hh:5 * hh + 5], axis=AX.X),
                     reads=[bPROB], writes=[bGSC])
            for blk in range(4):
                for hh in range(H):
                    C.op("dve", lambda h, blk=blk, hh=hh: h.tensor_scalar(
                        out=pv[:, blk, 5 * hh:5 * hh + 5], in0=pv[:, blk, 5 * hh:5 * hh + 5],
                        scalar1=gsc_f[:, blk, hh:hh + 1], scalar2=sc_f[:, hh:hh + 1],
                        op0=ALU.subtract, op1=ALU.mult),
                        reads=[bGSC, buf("sc")], writes=[bPROB])
            C.op("act", lambda h: h.activation(pv[:, :, 0:20], pv[:, :, 0:20], AF.Exp),
                 reads=[bPROB], writes=[bPROB])
            for hh in range(H):
                C.op("dve", lambda h, hh=hh: h.reduce_sum(gsc_f[:, :, 4 + hh:5 + hh],
                                                          pv[:, :, 5 * hh:5 * hh + 5], axis=AX.X),
                     reads=[bPROB], writes=[bGSC])
            C.op("pool", lambda h: h.tensor_copy(gsc_f[:, 0:2, 4:8], gsc_f[:, 0:2, 4:8]),
                 reads=[bGSC], writes=[bGSC])
            C.op("pool", lambda h: h.tensor_copy(gsc_f[:, 2:4, 4:8], gsc_f[:, 2:4, 4:8]),
                 reads=[bGSC], writes=[bGSC])
            C.op("dve", lambda h: h.reciprocal(out=gsc_f[:, :, 4:8], in_=gsc_f[:, :, 4:8]),
                 reads=[bGSC], writes=[bGSC])
            C.op("pool", lambda h: h.tensor_tensor(out=gsc_f[:, :, 4:8], in0=gsc_f[:, :, 4:8],
                                                   in1=sc_f[:, 4:8].unsqueeze(1).broadcast_to(
                                                       [P, 4, 4]), op=ALU.mult),
                 reads=[bGSC, buf("sc")], writes=[bGSC])
            for blk in range(4):
                for hh in range(H):
                    C.op("dve", lambda h, blk=blk, hh=hh: h.scalar_tensor_tensor(
                        out=pv[:, blk, 5 * hh:5 * hh + 5], in0=pv[:, blk, 5 * hh:5 * hh + 5],
                        scalar=gsc_f[:, blk, 4 + hh:5 + hh],
                        in1=sc_f[:, 8 + hh:9 + hh].broadcast_to([P, 5]),
                        op0=ALU.mult, op1=ALU.add),
                        reads=[bGSC, buf("sc")], writes=[bPROB])

            if u == 0 and s == 0:
                dump("prob", prob_t[:].rearrange("p a b -> p (a b)"), [bPROB])
                dump("wq", wq_b[:].rearrange("p a b -> p (a b)"), [wbufs["wq"]])
                dump("x_fm", x_fm[:].rearrange("p a b -> p (a b)"), [bXFM])
                dump("sc", sc_f[:], [buf("sc")])
                dump("hrow", hrow_f[:], [buf("hrow")])
            for pj, (w_t, wb) in enumerate(((wq_b, wbufs["wq"]), (wk_b, wbufs["wk"]),
                                            (wv_b, wbufs["wv"]))):
                C.op("dve", lambda h, pj=pj: h.tensor_copy(pre_b[:, :, 0:4],
                                                           hist_b[:, pj, :, :]),
                     reads=[bHIST], writes=[bPRE])
                for ic in range(NKC):
                    pp = ic % 2
                    for kc in range(NKC):
                        C.op("pe", lambda h, w_t=w_t, ic=ic, kc=kc, pp=pp: h.matmul(
                            ps_proj[pp][:], w_t[:, kc, ic * P:ic * P + P], x_fm[:, kc, :],
                            start=(kc == 0), stop=(kc == NKC - 1)),
                            reads=[wb, bXFM], writes=[bPS_PROJ[pp]])
                    if u == 0 and s == 0 and pj == 0 and ic == 0 and "psq" in DEBUG_TAPS:
                        C.op("act", lambda h: h.copy(sq_scr[:], ps_proj[0][:]),
                             reads=[bPS_PROJ[0]], writes=[bSQ])
                        dump("psq", sq_scr[:], [bSQ])
                    C.op("act", lambda h, ic=ic, pp=pp: h.copy(pre_b[:, ic, 4:4 + SLAB],
                                                               ps_proj[pp][:]),
                         reads=[bPS_PROJ[pp]], writes=[bPRE])
                    if u == 0 and s == 0 and pj == 0 and ic == 0:
                        dump("pre00", pre_b[:, 0, :], [bPRE])
                C.op("dve", lambda h, pj=pj: h.tensor_copy(hist_b[:, pj, :, :],
                                                           pre_b[:, :, SLAB:SLAB + 4]),
                     reads=[bPRE], writes=[bHIST])
                if u == 0 and s == 0 and pj == 0:
                    dump("pre02", pre_b[:, 0, :], [bPRE])
                import os as _os2
                for kc in range(NKC):
                    cps = ps_conv if kc % 2 == 0 else ps_u
                    bcps = [bPS_CONV] if kc % 2 == 0 else [bPS_U0, bPS_U1]
                    if not _os2.environ.get("STREAM_OFF"):
                        sl = stream_diags((pj * 8 + kc) * 4, 4)
                    if u == 0 and s == 0 and pj == 0 and kc == 1:
                        dump("pre03", pre_b[:, 0, :], [bPRE])
                        dump("pre05", pre_b[:, 0, :], [bPRE])
                        dump("pre06", pre_b[:, 0, :], [bPRE])
                    if not _os2.environ.get("CONV_OFF"):
                        for j in range(4):
                            C.op("pe", lambda h, sl=sl, j=j, kc=kc, cps=cps: h.matmul(
                                cps[:], dstr[sl][:, j, :],
                                pre_b[:, kc, j + 1:j + 1 + SLAB],
                                start=(j == 0), stop=(j == 3)),
                                reads=[bDS[sl], bPRE], writes=bcps)
                    if pj == 0:
                        dst, bdst = q_sb[:, kc, :], bQ
                    elif pj == 1:
                        dst, bdst = k_sb[:, kc, :], bK
                    else:
                        dst, bdst = vwin[:, kc, 64 + s * SLAB:64 + s * SLAB + SLAB], bVW
                    import os as _os
                    if _os.environ.get("SILU_SCRATCH"):
                        dst, bdst = o_tm[:, 0, 0:SLAB], buf("o_tm")
                    C.op("act", lambda h, dst=dst, cps=cps: h.activation(dst, cps[:], AF.Silu),
                         reads=bcps, writes=[bdst])
                if u == 0 and s == 0:
                    dump("q_sb" if pj == 0 else ("k_sb" if pj == 1 else "vwin"),
                         (q_sb if pj == 0 else (k_sb if pj == 1 else vwin))[:].rearrange(
                             "p a b -> p (a b)"), [bdst])
                    dump("pre" + str(pj), pre_b[:, 0, :], [bPRE])

        def phase_D_pre(u, b):
            s = b // 4
            i0 = (b % 4) * P
            for kc in range(NKC):
                C.op("pe", lambda h, kc=kc: h.transpose(
                    ps_tr[:, kc, :], q_sb[:, kc, i0:i0 + P], masks[:, 0, :]),
                    reads=[bQ, buf("masks")], writes=[bPS_TR])
            for hh in range(H):
                C.op("act", lambda h, hh=hh: h.activation(
                    sq_scr[:, 0:DK], ps_tr[:, 2 * hh:2 * hh + 2, :].rearrange(
                        "p a b -> p (a b)"), AF.Square,
                    accum_out=stat_t[:, b, hh:hh + 1]), reads=[bPS_TR], writes=[bSTAT, bSQ])
            yield
            for kc in range(NKC):
                C.op("pe", lambda h, kc=kc: h.transpose(
                    ps_tr[:, kc, :], k_sb[:, kc, i0:i0 + P], masks[:, 0, :]),
                    reads=[bK, buf("masks")], writes=[bPS_TR])
            for hh in range(H):
                C.op("act", lambda h, hh=hh: h.activation(
                    sq_scr[:, 0:DK], ps_tr[:, 2 * hh:2 * hh + 2, :].rearrange(
                        "p a b -> p (a b)"), AF.Square,
                    accum_out=stat_t[:, b, 4 + hh:5 + hh]), reads=[bPS_TR], writes=[bSTAT, bSQ])
            yield
            quake_rsqrt(stat_t[:, b, 8:16], stat_t[:, b, 0:8], 1e-6,
                        tiny_f[:, 0:8], tiny_f[:, 8:16], [bSTAT], nr=2)
            yield
            C.op("pool", lambda h: h.tensor_tensor(out=stat_t[:, b, 16:20],
                                                   in0=stat_t[:, b, 12:16],
                                                   in1=prob_t[:, b, 20:24], op=ALU.mult),
                 reads=[bSTAT, bPROB], writes=[bSTAT])
            C.op("dve", lambda h: h.tensor_scalar(out=stat_t[:, b, 20:24],
                                                  in0=stat_t[:, b, 16:20], scalar1=-1.0,
                                                  scalar2=None, op0=ALU.mult),
                 reads=[bSTAT], writes=[bSTAT])
            par = b % 2
            kpos_s, kbn_s, vb_s = kpos_s2[par], kbn_s2[par], vb_s2[par]
            bKPOS, bKBN, bVB = bKPOS2[par], bKBN2[par], bVB2[par]
            for hh in range(H):
                hs = slice(hh * DK, hh * DK + DK)
                pt = ps_tr[:, 2 * hh:2 * hh + 2, :].rearrange("p a b -> p (a b)")
                C.op("dve", lambda h, hh=hh, hs=hs, pt=pt, kpos_s=kpos_s: h.tensor_scalar(
                    out=kpos_s[:, hs], in0=pt,
                    scalar1=stat_t[:, b, 12 + hh:13 + hh], scalar2=None, op0=ALU.mult),
                    reads=[bPS_TR, bSTAT], writes=[bKPOS])
                C.op("dve", lambda h, hh=hh, hs=hs, pt=pt, kbn_s=kbn_s: h.tensor_scalar(
                    out=kbn_s[:, hs], in0=pt,
                    scalar1=stat_t[:, b, 20 + hh:21 + hh], scalar2=None, op0=ALU.mult),
                    reads=[bPS_TR, bSTAT], writes=[bKBN])
            yield
            w0 = 64 + b * P
            for kc in range(NKC):
                C.op("pe", lambda h, kc=kc: h.transpose(
                    ps_tr[:, kc, :], vwin[:, kc, w0:w0 + P], masks[:, 0, :]),
                    reads=[bVW, buf("masks")], writes=[bPS_TR])
            for hh in range(H):
                hs = slice(hh * DK, hh * DK + DK)
                pt = ps_tr[:, 2 * hh:2 * hh + 2, :].rearrange("p a b -> p (a b)")
                C.op("dve", lambda h, hh=hh, hs=hs, pt=pt, vb_s=vb_s: h.tensor_scalar(
                    out=vb_s[:, hs], in0=pt,
                    scalar1=prob_t[:, b, 20 + hh:21 + hh], scalar2=None, op0=ALU.mult),
                    reads=[bPS_TR, bPROB], writes=[bVB])
            yield
            for hh in range(H):
                for j in range(2):
                    C.op("pe", lambda h, hh=hh, j=j: h.matmul(
                        ps_g[:, hh, :], k_sb[:, 2 * hh + j, i0:i0 + P],
                        k_sb[:, 2 * hh + j, i0:i0 + P], start=(j == 0), stop=(j == 1)),
                        reads=[bK], writes=[bPS_G])
            for hh in range(H):
                C.op("dve", lambda h, hh=hh: h.tensor_scalar(
                    out=zz_s[:, hh, :], in0=ps_g[:, hh, :],
                    scalar1=stat_t[:, b, 12 + hh:13 + hh], scalar2=None, op0=ALU.mult),
                    reads=[bPS_G, bSTAT], writes=[bZZ])
            for hh in range(H):
                C.op("pe", lambda h, hh=hh: h.transpose(
                    ps_tr[:, hh, :], zz_s[:, hh, :], masks[:, 0, :]),
                    reads=[bZZ, buf("masks")], writes=[bPS_TR])
            for hh in range(H):
                C.op("dve", lambda h, hh=hh: h.scalar_tensor_tensor(
                    out=a_s[:, hh, :], in0=ps_tr[:, hh, :],
                    scalar=stat_t[:, b, 20 + hh:21 + hh], in1=masks[:, 1, :],
                    op0=ALU.mult, op1=ALU.mult),
                    reads=[bPS_TR, bSTAT, buf("masks")], writes=[bA])
            for hh in range(H):
                C.op("dve", lambda h, hh=hh: h.tensor_scalar(
                    out=ww_s[:, hh, :], in0=ps_g[:, hh, :],
                    scalar1=stat_t[:, b, 20 + hh:21 + hh], scalar2=None, op0=ALU.mult),
                    reads=[bPS_G, bSTAT], writes=[bWW])
            for hh in range(H):
                C.op("pe", lambda h, hh=hh: h.transpose(
                    ps_tr[:, 4 + hh, :], ww_s[:, hh, :], masks[:, 0, :]),
                    reads=[bWW, buf("masks")], writes=[bPS_TR])
            for hh in range(H):
                C.op("dve", lambda h, hh=hh: h.scalar_tensor_tensor(
                    out=at_s[:, hh, :], in0=ps_tr[:, 4 + hh, :],
                    scalar=stat_t[:, b, 12 + hh:13 + hh], in1=masks[:, 2, :],
                    op0=ALU.mult, op1=ALU.mult),
                    reads=[bPS_TR, bSTAT, buf("masks")], writes=[bAT])
            yield
            PC, PN, QC, QN = 0, 1, 2, 3
            RC, RN, RTC, RTN = 0, 1, 2, 3
            C.op("dve", lambda h, pc=PC: h.tensor_copy(pq_s[pc][:], at_s[:]), reads=[bAT],
                 writes=[bPQ[PC]])
            C.op("dve", lambda h, qc=QC: h.tensor_copy(pq_s[qc][:], a_s[:]), reads=[bA],
                 writes=[bPQ[QC]])
            for hh in range(H):
                C.op("dve", lambda h, hh=hh, rc=RC, pc=PC: h.tensor_tensor(
                    out=rr_s[rc][:, hh, :], in0=pq_s[pc][:, hh, :], in1=masks[:, 0, :],
                    op=ALU.add), reads=[bPQ[PC], buf("masks")], writes=[bRR[RC]])
                C.op("dve", lambda h, hh=hh, rtc=RTC, qc=QC: h.tensor_tensor(
                    out=rr_s[rtc][:, hh, :], in0=pq_s[qc][:, hh, :], in1=masks[:, 0, :],
                    op=ALU.add), reads=[bPQ[QC], buf("masks")], writes=[bRR[RTC]])
            yield
            for kstep in range(1, 7):
                yield
                for hh in range(H):
                    C.op("pe", lambda h, hh=hh, qc=QC, pc=PC: h.matmul(
                        ps_g[:, hh, :], pq_s[qc][:, hh, :], pq_s[pc][:, hh, :],
                        start=True, stop=True), reads=[bPQ[QC], bPQ[PC]], writes=[bPS_G])
                C.op("dve", lambda h, pn=PN: h.tensor_copy(pq_s[pn][:], ps_g[:]),
                     reads=[bPS_G], writes=[bPQ[PN]])
                if kstep <= 5:
                    for hh in range(H):
                        C.op("pe", lambda h, hh=hh, pc=PC, qc=QC: h.matmul(
                            ps_tr32[:, hh, :], pq_s[pc][:, hh, :], pq_s[qc][:, hh, :],
                            start=True, stop=True), reads=[bPQ[PC], bPQ[QC]], writes=[bPS_TR])
                    C.op("act", lambda h, qn=QN: h.copy(pq_s[qn][:], ps_tr32[:]),
                         reads=[bPS_TR], writes=[bPQ[QN]])
                for hh in range(H):
                    C.op("pe", lambda h, hh=hh, rtc=RTC, pn=PN: h.matmul(
                        ps_g[:, hh, :], rr_s[rtc][:, hh, :], pq_s[pn][:, hh, :],
                        start=True, stop=True), reads=[bRR[RTC], bPQ[PN]], writes=[bPS_G])
                C.op("dve", lambda h, rn=RN, rc=RC: h.tensor_tensor(
                    out=rr_s[rn][:], in0=ps_g[:], in1=rr_s[rc][:], op=ALU.add),
                     reads=[bPS_G, bRR[RC]], writes=[bRR[RN]])
                if kstep <= 5:
                    for hh in range(H):
                        C.op("pe", lambda h, hh=hh, pn=PN, rtc=RTC: h.matmul(
                            ps_tr32[:, hh, :], pq_s[pn][:, hh, :], rr_s[rtc][:, hh, :],
                            start=True, stop=True), reads=[bPQ[PN], bRR[RTC]], writes=[bPS_TR])
                    C.op("act", lambda h, rtn=RTN: h.copy(rr_s[rtn][:], ps_tr32[:]),
                         reads=[bPS_TR], writes=[bRR[RTN]])
                    C.op("dve", lambda h, rtn=RTN, rtc=RTC: h.tensor_tensor(
                        out=rr_s[rtn][:], in0=rr_s[rtn][:], in1=rr_s[rtc][:], op=ALU.add),
                         reads=[bRR[RTN], bRR[RTC]], writes=[bRR[RTN]])
                if u == 0 and b == 0:
                    dump(f"p{kstep}", pq_s[PN][:].rearrange("p a b -> p (a b)"), [bPQ[PN]])
                    dump(f"r{kstep}", rr_s[RN][:].rearrange("p a b -> p (a b)"), [bRR[RN]])
                PC, PN = PN, PC
                QC, QN = QN, QC
                RC, RN = RN, RC
                RTC, RTN = RTN, RTC
            phase_D_pre.rfin_rc = RC

        def phase_D_tail(u, b):
            i0 = (b % 4) * P
            for hh in range(H):
                for j in range(2):
                    C.op("pe", lambda h, hh=hh, j=j: h.matmul(
                        ps_g[:, hh, :], k_sb[:, 2 * hh + j, i0:i0 + P],
                        q_sb[:, 2 * hh + j, i0:i0 + P], start=(j == 0), stop=(j == 1)),
                        reads=[bK, bQ], writes=[bPS_G])
            for hh in range(H):
                C.op("dve", lambda h, hh=hh: h.scalar_tensor_tensor(
                    out=bt_s[:, hh, :], in0=ps_g[:, hh, :],
                    scalar=stat_t[:, b, 12 + hh:13 + hh], in1=masks[:, 3, :],
                    op0=ALU.mult, op1=ALU.mult),
                    reads=[bPS_G, bSTAT, buf("masks")], writes=[bBT])
            rc = phase_D_pre.rfin_rc
            C.op("act", lambda h, rc=rc: h.copy(rfin_s[:], rr_s[rc][:]),
                 reads=[bRR[rc]], writes=[buf("rfin")])
            if u == 0 and b == 0:
                dump("rfin", rfin_s[:].rearrange("p a b -> p (a b)"), [buf("rfin")])

        def phase_D_scan(u, b):
            s = b // 4
            i0 = (b % 4) * P
            par = b % 2
            kpos_s, kbn_s, vb_s = kpos_s2[par], kbn_s2[par], vb_s2[par]
            bKPOS, bKBN, bVB = bKPOS2[par], bKBN2[par], bVB2[par]
            for hh in range(H):
                yield
                hs = slice(hh * DK, hh * DK + DK)
                for j in range(2):
                    C.op("pe", lambda h, hh=hh, j=j: h.matmul(
                        ps_conv[:, j * P:j * P + P],
                        kbn_s[:, hh * DK + j * P:hh * DK + j * P + P],
                        rfin_s[:, hh, :], start=True, stop=True),
                        reads=[bKBN, buf("rfin")], writes=[bPS_CONV])
                C.op("dve", lambda h: h.tensor_copy(
                    wn_s[:].rearrange("p a b -> p (a b)"), ps_conv[:, 0:2 * P]),
                     reads=[bPS_CONV], writes=[bWN])
                yield
                C.op("pe", lambda h, hh=hh, hs=hs: h.matmul(
                    ps_u[:, 0:DK], rfin_s[:, hh, :], vb_s[:, hs], start=True, stop=False),
                    reads=[buf("rfin"), bVB], writes=[bPS_U0])
                for j in range(2):
                    C.op("pe", lambda h, hh=hh, j=j: h.matmul(
                        ps_u[:, 0:DK], wn_s[:, j, :], S_b[:, j, hh, :],
                        start=False, stop=(j == 1)), reads=[bWN, bSB4[hh]], writes=[bPS_U0])
                C.op("dve", lambda h, hh=hh: h.tensor_copy(uadj_s[:, hh, :], ps_u[:, 0:DK]),
                     reads=[bPS_U0], writes=[bUADJ])
                yield
                for j in range(2):
                    C.op("pe", lambda h, hh=hh, j=j: h.matmul(
                        ps_fir[:, 0:DK], q_sb[:, 2 * hh + j, i0:i0 + P], S_b[:, j, hh, :],
                        start=(j == 0), stop=False), reads=[bQ, bSB4[hh]], writes=[bPS_FIR])
                C.op("pe", lambda h, hh=hh: h.matmul(
                    ps_fir[:, 0:DK], bt_s[:, hh, :], uadj_s[:, hh, :],
                    start=False, stop=True), reads=[bBT, bUADJ], writes=[bPS_FIR])
                C.op("dve", lambda h, hh=hh, hs=hs: h.tensor_scalar(
                    out=o_tm[:, b, hs], in0=ps_fir[:, 0:DK],
                    scalar1=stat_t[:, b, 8 + hh:9 + hh], scalar2=None, op0=ALU.mult),
                    reads=[bPS_FIR, bSTAT], writes=[bOTM])
                yield
                for j in range(2):
                    C.op("pe", lambda h, hh=hh, j=j: h.matmul(
                        ps_proj[0][:, j * DK:j * DK + DK],
                        kpos_s[:, hh * DK + j * P:hh * DK + j * P + P], uadj_s[:, hh, :],
                        start=True, stop=True), reads=[bKPOS, bUADJ], writes=[bPS_PROJ[0]])
                for j in range(2):
                    C.op("dve", lambda h, hh=hh, j=j: h.tensor_tensor(
                        out=S_f[:, j, hh, :], in0=S_f[:, j, hh, :],
                        in1=ps_proj[0][:, j * DK:j * DK + DK],
                        op=ALU.add), reads=[bPS_PROJ[0], bSF4[hh]], writes=[bSF4[hh]])
                for j in range(2):
                    C.op("act", lambda h, hh=hh, j=j: h.copy(S_b[:, j, hh, :],
                                                             S_f[:, j, hh, :]),
                         reads=[bSF4[hh]], writes=[bSB4[hh]])

            if u == 0 and b == 1:
                dump("uadj1", uadj_s[:].rearrange("p a b -> p (a b)"), [bUADJ])
                dump("od1", o_tm[:, 1, :], [bOTM])
                dump("S2", S_f[:].rearrange("p a b c -> p (a b c)"), list(bSF4))
            if u == 0 and b == 0:
                dump("stat", stat_t[:].rearrange("p a b -> p (a b)"), [bSTAT])
                dump("a_s", a_s[:].rearrange("p a b -> p (a b)"), [bA])
                dump("at_s", at_s[:].rearrange("p a b -> p (a b)"), [bAT])
                dump("bt_s", bt_s[:].rearrange("p a b -> p (a b)"), [bBT])
                dump("uadj", uadj_s[:].rearrange("p a b -> p (a b)"), [bUADJ])
                dump("kpos", kpos_s[:], [bKPOS])
                dump("vb", vb_s[:], [bVB])
                dump("o_d", o_tm[:, 0, :], [bOTM])
                dump("S1", S_f[:].rearrange("p a b c -> p (a b c)"), list(bSF4))

        def phase_FC(u, hh):
            for ci, kc in enumerate((2 * hh, 2 * hh + 1)):
                for br in range(3):
                    K = FIR_K[br]
                    base = 96 + kc * NTAP_FIR + FIR_OFF[br]
                    tdone = 0
                    while tdone < K:
                        g = min(8, K - tdone)
                        sl = stream_diags(base + tdone, g)
                        for jj in range(g):
                            j = tdone + jj
                            c0 = 64 + j - (K - 1)
                            for half in range(2):
                                C.op("pe", lambda h, sl=sl, jj=jj, kc=kc, c0=c0, half=half,
                                     j=j, K=K: h.matmul(
                                         ps_fir[:, half * SLAB:half * SLAB + SLAB],
                                         dstr[sl][:, jj, :],
                                         vwin[:, kc, c0 + half * SLAB:c0 + half * SLAB + SLAB],
                                         start=(j == 0), stop=(j == K - 1)),
                                     reads=[bDS[sl], bVW], writes=[bPS_FIR])
                        tdone += g
                    C.op("dve", lambda h, br=br, ci=ci: h.tensor_copy(
                        fb[:, br, ci, :], ps_fir[:]), reads=[bPS_FIR], writes=[bFB])
            for blk in range(8):
                pt, bpt = tr_slot()
                for br in range(3):
                    for ci in range(2):
                        C.op("pe", lambda h, br=br, ci=ci, blk=blk, pt=pt: h.transpose(
                            pt[:, br * 2 + ci, :], fb[:, br, ci, blk * P:blk * P + P],
                            masks[:, 0, :]), reads=[bFB, buf("masks")], writes=[bpt])
                for ci in range(2):
                    C.op("pe", lambda h, ci=ci, blk=blk, hh=hh, pt=pt: h.transpose(
                        pt[:, 6 + ci, :], vwin[:, 2 * hh + ci, 64 + blk * P:64 + blk * P + P],
                        masks[:, 0, :]), reads=[bVW, buf("masks")], writes=[bpt])
                pvb = prob_t[:, blk, :]
                C.op("dve", lambda h, blk=blk, hh=hh, pvb=pvb, pt=pt: h.tensor_scalar(
                    out=acc_f[:, blk, :], in0=pt[:, 0:2, :].rearrange("p a b -> p (a b)"),
                    scalar1=pvb[:, 5 * hh + 0:5 * hh + 1], scalar2=None, op0=ALU.mult),
                    reads=[bpt, bPROB], writes=[bACC])
                for br in (1, 2):
                    C.op("dve", lambda h, blk=blk, hh=hh, br=br, pvb=pvb, pt=pt:
                         h.scalar_tensor_tensor(
                        out=acc_f[:, blk, :],
                        in0=pt[:, 2 * br:2 * br + 2, :].rearrange("p a b -> p (a b)"),
                        scalar=pvb[:, 5 * hh + br:5 * hh + br + 1],
                        in1=acc_f[:, blk, :], op0=ALU.mult, op1=ALU.add),
                        reads=[bpt, bPROB, bACC], writes=[bACC])
                C.op("dve", lambda h, blk=blk, hh=hh, pvb=pvb: h.scalar_tensor_tensor(
                    out=acc_f[:, blk, :], in0=o_tm[:, blk, hh * DK:hh * DK + DK],
                    scalar=pvb[:, 5 * hh + 3:5 * hh + 4], in1=acc_f[:, blk, :],
                    op0=ALU.mult, op1=ALU.add), reads=[bOTM, bPROB, bACC], writes=[bACC])
                C.op("dve", lambda h, blk=blk, hh=hh, pvb=pvb, pt=pt: h.scalar_tensor_tensor(
                    out=acc_f[:, blk, :], in0=pt[:, 6:8, :].rearrange("p a b -> p (a b)"),
                    scalar=pvb[:, 5 * hh + 4:5 * hh + 5], in1=acc_f[:, blk, :],
                    op0=ALU.mult, op1=ALU.add), reads=[bpt, bPROB, bACC], writes=[bACC])
                C.op("act", lambda h, blk=blk, hh=hh: h.activation(
                    sq_scr[:, 0:DK], acc_f[:, blk, :], AF.Square,
                    accum_out=rms_f[:, blk, hh:hh + 1]), reads=[bACC], writes=[bRMS, bSQ])
            C.op("dve", lambda h, hh=hh: h.tensor_scalar(
                out=tiny_f[:, 0:8], in0=rms_f[:, :, hh], scalar1=1.0 / DK, scalar2=None,
                op0=ALU.mult), reads=[bRMS, buf("quake_scr")], writes=[buf("quake_scr")])
            quake_rsqrt(rms_f[:, :, hh], tiny_f[:, 0:8], 1e-5,
                        tiny_f[:, 8:16], tiny_f[:, 0:8].bitcast(F32), [bRMS], nr=2)
            for blk in range(8):
                C.op("dve", lambda h, blk=blk, hh=hh: h.tensor_scalar(
                    out=o_tm[:, blk, hh * DK:hh * DK + DK], in0=acc_f[:, blk, :],
                    scalar1=rms_f[:, blk, hh:hh + 1], scalar2=None, op0=ALU.mult),
                    reads=[bACC, bRMS], writes=[bOTM])

            if u == 0 and hh == 0:
                dump("fb", fb[:].rearrange("p a b c -> p (a b c)"), [bFB])
                dump("o_c", o_tm[:, 0, :], [bOTM])
                dump("rms", rms_f[:].rearrange("p a b -> p (a b)"), [bRMS])

        def phase_W(u, s):
            gslab = u * 2 + s
            t0 = gslab * SLAB
            for blk in range(4):
                gb = s * 4 + blk
                pt, bpt = tr_slot()
                for kc in range(NKC):
                    C.op("pe", lambda h, gb=gb, kc=kc, pt=pt: h.transpose(
                        pt[:, kc, :], o_tm[:, gb, kc * P:kc * P + P], masks[:, 0, :]),
                        reads=[bOTM, buf("masks")], writes=[bpt])
                C.op("dve", lambda h, blk=blk, pt=pt: h.tensor_copy(
                    fm_scr[:, :, blk * P:blk * P + P], pt[:]), reads=[bpt], writes=[bFM])
            for blk in range(4):
                tb = t0 + blk * P
                xb = bXF[blk % 2]
                dma("sp", x_f32[blk % 2][:], x_hbm[tb:tb + P, :], writes=[xb],
                    sem_eng="ld0")
                for ih in range(2):
                    for kc in range(NKC):
                        C.op("pe", lambda h, blk=blk, ih=ih, kc=kc: h.matmul(
                            ps_proj[ih][:], fm_scr[:, kc, blk * P:blk * P + P],
                            wo_b[:, kc, ih * SLAB:ih * SLAB + SLAB],
                            start=(kc == 0), stop=(kc == NKC - 1)),
                            reads=[bFM, wbufs["wo"]], writes=[bPS_PROJ[ih]])
                for ih in range(2):
                    C.op("dve", lambda h, blk=blk, ih=ih: h.tensor_tensor(
                        out=y1_f[:, ih * SLAB:ih * SLAB + SLAB], in0=ps_proj[ih][:],
                        in1=x_f32[blk % 2][:, ih * SLAB:ih * SLAB + SLAB], op=ALU.add),
                        reads=[bPS_PROJ[ih], xb], writes=[bY1F])
                C.op("dve", lambda h: h.reduce_sum(ln_f[:, 0:1], y1_f[:], axis=AX.X),
                     reads=[bY1F], writes=[bLN, bSTA])
                for ih in range(2):
                    C.op("act", lambda h, ih=ih: h.activation(
                        sq_scr[:], y1_f[:, ih * SLAB:ih * SLAB + SLAB], AF.Square,
                        accum_out=ln_f[:, 1 + 2 * ih:2 + 2 * ih]),
                        reads=[bY1F], writes=[bLN, bSQ, bSTA])
                C.op("dve", lambda h: h.tensor_tensor(out=ln_f[:, 1:2], in0=ln_f[:, 1:2],
                                                      in1=ln_f[:, 3:4], op=ALU.add),
                     reads=[bLN], writes=[bLN])
                C.op("pool", lambda h: h.tensor_scalar(out=ln_f[:, 0:1], in0=ln_f[:, 0:1],
                                                       scalar1=1.0 / D, scalar2=None,
                                                       op0=ALU.mult), reads=[bLN], writes=[bLN])
                C.op("dve", lambda h: h.tensor_tensor(out=ln_f[:, 2:3], in0=ln_f[:, 0:1],
                                                      in1=ln_f[:, 0:1], op=ALU.mult),
                     reads=[bLN], writes=[bLN])
                C.op("pool", lambda h: h.tensor_scalar(out=ln_f[:, 1:2], in0=ln_f[:, 1:2],
                                                       scalar1=1.0 / D, scalar2=None,
                                                       op0=ALU.mult), reads=[bLN], writes=[bLN])
                C.op("dve", lambda h: h.tensor_tensor(out=ln_f[:, 9:10], in0=ln_f[:, 1:2],
                                                      in1=ln_f[:, 2:3], op=ALU.subtract),
                     reads=[bLN], writes=[bLN])
                quake_rsqrt(ln_f[:, 1:2], ln_f[:, 9:10], 1e-5,
                            tiny_f[:, 0:1], tiny_f[:, 1:2], [bLN], nr=2)
                C.op("dve", lambda h, blk=blk: h.tensor_scalar(
                    out=y1n_t[:, blk, :], in0=y1_f[:], scalar1=ln_f[:, 0:1],
                    scalar2=ln_f[:, 1:2], op0=ALU.subtract, op1=ALU.mult),
                    reads=[bY1F, bLN], writes=[bY1N])

            if u == 0 and s == 0:
                dump("y1n", y1n_t[:].rearrange("p a b -> p (a b)"), [bY1N])

        def phase_N(u, s):
            gslab = u * 2 + s
            t0 = gslab * SLAB
            for blk in range(4):
                pt, bpt = tr_slot()
                for kc in range(NKC):
                    C.op("pe", lambda h, blk=blk, kc=kc, pt=pt: h.transpose(
                        pt[:, kc, :], y1n_t[:, blk, kc * P:kc * P + P], masks[:, 0, :]),
                        reads=[bY1N, buf("masks")], writes=[bpt])
                C.op("dve", lambda h, blk=blk, pt=pt: h.tensor_copy(
                    fm_scr[:, :, blk * P:blk * P + P], pt[:]), reads=[bpt], writes=[bFM])
            for kc in range(NKC):
                C.op("pe", lambda h, kc=kc: h.matmul(
                    ps_proj[0][0:64, :], win_b[:, kc, :], fm_scr[:, kc, :],
                    start=(kc == 0), stop=(kc == NKC - 1)),
                    reads=[buf("win_b"), bFM], writes=[bPS_PROJ[0]])
            C.op("dve", lambda h: h.tensor_scalar(out=ffs_f[:], in0=ps_proj[0][0:64, :],
                                                  scalar1=small_f[:, 49:50], scalar2=None,
                                                  op0=ALU.add), reads=[bPS_PROJ[0], buf("small")],
                 writes=[bFFS])
            for i in range(16):
                C.op("act", lambda h, i=i: h.activation(
                    ffh_f[:, 0, :], ffs_f[:], AF.Relu, bias=small_f[:, 16 + i:17 + i],
                    scale=small_f[:, i:i + 1]), reads=[bFFS, buf("small")], writes=[bFFH])
                if i == 0:
                    C.op("dve", lambda h, i=i: h.tensor_scalar(
                        out=ffg_f[:], in0=ffh_f[:, 0, :], scalar1=small_f[:, 32 + i:33 + i],
                        scalar2=small_f[:, 48:49], op0=ALU.mult, op1=ALU.add),
                        reads=[bFFH, buf("small")], writes=[bFFG])
                else:
                    C.op("dve", lambda h, i=i: h.scalar_tensor_tensor(
                        out=ffg_f[:], in0=ffh_f[:, 0, :], scalar=small_f[:, 32 + i:33 + i],
                        in1=ffg_f[:], op0=ALU.mult, op1=ALU.add),
                        reads=[bFFH, buf("small"), bFFG], writes=[bFFG])
            C.op("dve", lambda h: h.tensor_copy(ffg_b[0:64, :], ffg_f[:]),
                 reads=[bFFG], writes=[bFFGB])
            C.op("dve", lambda h: h.memset(ffg_b[64:65, :], 1.0), writes=[bFFGB])
            for blk in range(4):
                tb = t0 + blk * P
                yb = bY2[blk % 2]
                for ih in range(2):
                    C.op("pe", lambda h, blk=blk, ih=ih: h.matmul(
                        ps_proj[ih][:], ffg_b[0:65, blk * P:blk * P + P],
                        woutE_b[0:65, ih * SLAB:ih * SLAB + SLAB], start=True, stop=True),
                        reads=[bFFGB, buf("woutE")], writes=[bPS_PROJ[ih]])
                    C.op("dve", lambda h, blk=blk, ih=ih: h.tensor_tensor(
                        out=y2_f[blk % 2][:, ih * SLAB:ih * SLAB + SLAB], in0=ps_proj[ih][:],
                        in1=y1n_t[:, blk, ih * SLAB:ih * SLAB + SLAB], op=ALU.add),
                        reads=[bPS_PROJ[ih], bY1N], writes=[yb])
                C.op("dve", lambda h, blk=blk: h.reduce_sum(ln_f[:, 4:5], y2_f[blk % 2][:],
                                                            axis=AX.X), reads=[yb],
                     writes=[bLN, bSTA])
                for ih in range(2):
                    C.op("act", lambda h, blk=blk, ih=ih: h.activation(
                        sq_scr[:], y2_f[blk % 2][:, ih * SLAB:ih * SLAB + SLAB], AF.Square,
                        accum_out=ln_f[:, 5 + 2 * ih:6 + 2 * ih]),
                        reads=[yb], writes=[bLN, bSQ, bSTA])
                C.op("dve", lambda h: h.tensor_tensor(out=ln_f[:, 5:6], in0=ln_f[:, 5:6],
                                                      in1=ln_f[:, 7:8], op=ALU.add),
                     reads=[bLN], writes=[bLN])
                C.op("pool", lambda h: h.tensor_scalar(out=ln_f[:, 4:5], in0=ln_f[:, 4:5],
                                                       scalar1=1.0 / D, scalar2=None,
                                                       op0=ALU.mult), reads=[bLN], writes=[bLN])
                C.op("dve", lambda h: h.tensor_tensor(out=ln_f[:, 8:9], in0=ln_f[:, 4:5],
                                                      in1=ln_f[:, 4:5], op=ALU.mult),
                     reads=[bLN], writes=[bLN])
                C.op("pool", lambda h: h.tensor_scalar(out=ln_f[:, 5:6], in0=ln_f[:, 5:6],
                                                       scalar1=1.0 / D, scalar2=None,
                                                       op0=ALU.mult), reads=[bLN], writes=[bLN])
                C.op("dve", lambda h: h.tensor_tensor(out=ln_f[:, 10:11], in0=ln_f[:, 5:6],
                                                      in1=ln_f[:, 8:9], op=ALU.subtract),
                     reads=[bLN], writes=[bLN])
                quake_rsqrt(ln_f[:, 5:6], ln_f[:, 10:11], 1e-5,
                            tiny_f[:, 2:3], tiny_f[:, 3:4], [bLN], nr=2)
                C.op("dve", lambda h, blk=blk: h.tensor_scalar(
                    out=y2_f[blk % 2][:], in0=y2_f[blk % 2][:], scalar1=ln_f[:, 4:5],
                    scalar2=ln_f[:, 5:6], op0=ALU.subtract, op1=ALU.mult),
                    reads=[yb, bLN], writes=[yb])
                nonlocal_store(tb, blk, yb)

        store_count = [0]

        def nonlocal_store(tb, blk, yb):
            dma("pool", y_hbm[tb:tb + P, :], y2_f[blk % 2][:], reads=[yb], sem_eng="st")
            store_count[0] += 1

        for u in range(nsuper):
            for s in range(2):
                if "P" in PHASES:
                    phase_P(u, s)
                if "D" in PHASES:
                    def drain(*gens):
                        gens = [g for g in gens if g is not None]
                        while gens:
                            for g in list(gens):
                                try:
                                    next(g)
                                except StopIteration:
                                    gens.remove(g)
                    import os as _os3
                    b0 = 4 * s
                    if _os3.environ.get("NO_IL"):
                        for b in range(b0, b0 + 4):
                            drain(phase_D_pre(u, b))
                            phase_D_tail(u, b)
                            drain(phase_D_scan(u, b))
                    else:
                        drain(phase_D_pre(u, b0))
                        phase_D_tail(u, b0)
                        for b in range(b0 + 1, b0 + 4):
                            drain(phase_D_scan(u, b - 1), phase_D_pre(u, b))
                            phase_D_tail(u, b)
                        drain(phase_D_scan(u, b0 + 3))
            if "F" in PHASES:
                for hh in range(H):
                    phase_FC(u, hh)
            for s in range(2):
                if "W" in PHASES:
                    phase_W(u, s)
                if "N" in PHASES:
                    phase_N(u, s)
            C.op("dve", lambda h: h.tensor_copy(vwin[:, :, 0:64], vwin[:, :, SUPER:SUPER + 64]),
                 reads=[bVW], writes=[bVW])

        total_pool = C.e["pool"].n
        total_st = C.e["st"].n
        C.e["pool"].emit(lambda h, t=total_pool: h.wait_ge(sems["pool"], t))
        C.e["pool"].emit(lambda h, t=total_st: h.wait_ge(sems["st"], t))
        for qn in ("dg0", "dg1", "dg2", "dg3", "ld0", "ld1"):
            tq = C.e[qn].n
            if tq:
                C.e["pool"].emit(lambda h, t=tq, s=sems[qn]: h.wait_ge(s, t))

        def replay(name):
            def f(h):
                for fn in C.e[name].script:
                    fn(h)
            return f

        blk_g.sync(replay("sp"))
        blk_g.tensor(replay("pe"))
        blk_g.vector(replay("dve"))
        blk_g.scalar(replay("act"))
        blk_g.gpsimd(replay("pool"))

    return nc


def _host_inputs(inputs, L):
    a = inputs
    wq = np.ascontiguousarray(a["L0_attn_Wq"].T)
    wk = np.ascontiguousarray(a["L0_attn_Wk"].T)
    wv = np.ascontiguousarray(a["L0_attn_Wv"].T)
    wo = np.ascontiguousarray(a["L0_attn_Wo"].T)
    gb = np.concatenate([a["L0_attn_gate_w"].reshape(20, D), a["L0_attn_Wb"]], 0)
    gbT = np.ascontiguousarray(gb.T)
    gb_bias = np.concatenate([a["L0_attn_gate_b"].reshape(20), np.zeros(4, np.float32)])
    gb_bias = np.ascontiguousarray(gb_bias.reshape(24, 1).astype(np.float32))
    convw = np.ascontiguousarray(np.concatenate(
        [a["L0_attn_conv_q"], a["L0_attn_conv_k"], a["L0_attn_conv_v"]], 1))
    firw = np.ascontiguousarray(np.concatenate(
        [a["L0_attn_fir_s"].reshape(D, 3), a["L0_attn_fir_m"].reshape(D, 15),
         a["L0_attn_fir_l"].reshape(D, 63)], 1))
    win = np.ascontiguousarray(a["L0_ffn_Win"])
    woutE = np.ascontiguousarray(np.concatenate(
        [a["L0_ffn_Wout"], a["L0_ffn_bout"].reshape(1, D)], 0))
    smalls = np.zeros((64, 50), np.float32)
    smalls[:, 0:16] = a["L0_ffn_W1"]
    smalls[:, 16:32] = a["L0_ffn_b1"]
    smalls[:, 32:48] = a["L0_ffn_W2"]
    smalls[:, 48] = a["L0_ffn_b2"]
    smalls[:, 49] = a["L0_ffn_bin_"]
    nw = np.ascontiguousarray(a["L0_attn_norm_w"].reshape(2, P).T)
    hrow = np.zeros((1, 8), np.float32)
    hrow[0, 0:4] = a["L0_attn_log_temp"].reshape(4)
    hrow[0, 4:8] = a["L0_attn_eps_param"].reshape(4)
    common = dict(wqT=wq, wkT=wk, wvT=wv, woT=wo, gbT=gbT, gb_bias=gb_bias, convw=convw,
                  firw=firw, win=win, woutE=woutE, smalls=smalls, nw=nw, hrow=hrow)
    x = a["L0_attn_hidden_states"]
    maps = []
    for c in range(x.shape[0]):
        m = dict(common)
        m["x"] = np.ascontiguousarray(x[c, :L, :])
        maps.append(m)
    return maps


def kernel(**inputs):
    L = inputs["L0_attn_hidden_states"].shape[1]
    ncores = inputs["L0_attn_hidden_states"].shape[0]
    nc = build_core(L=L, NCORES=ncores)
    maps = _host_inputs(inputs, L)
    res = run_bass_kernel_spmd(nc, maps, core_ids=list(range(ncores)))
    out = np.stack([r["y"] for r in res.results], 0)
    return out.astype(np.float32)


# version 09
# speedup vs baseline: 1.1802x; 1.1802x over previous
import contextlib
import numpy as np

import concourse.bass as bass
import concourse.mybir as mybir
from concourse.bass_utils import run_bass_kernel_spmd

F32 = mybir.dt.float32
BF16 = mybir.dt.bfloat16
AF = mybir.ActivationFunctionType
ALU = mybir.AluOpType
AX = mybir.AxisListType

P = 128
D = 1024
H = 4
DK = 256
NKC = 8
SLAB = 512
SUPER = 1024
FIR_K = (3, 15, 63)
FIR_OFF = (0, 3, 18)
NTAP_FIR = 81
QUAKE = 0x5F3759DF


class Eng:
    def __init__(self, name, sem):
        self.name = name
        self.sem = sem
        self.n = 0
        self.waited = {}
        self.script = []

    def emit(self, fn):
        self.script.append(fn)

    def mile_next(self, k=1):
        self.n += k
        return self.n


class Buf:
    __slots__ = ("ready", "readers", "name")

    def __init__(self, name="?"):
        self.name = name
        self.ready = {}
        self.readers = {}


class Ctx:

    def __init__(self, nc, sems):
        self.nc = nc
        self.e = {n: Eng(n, sems[n]) for n in ("pe", "dve", "act", "pool", "sp",
                                               "ld0", "ld1", "dg0", "dg1", "dg2", "dg3", "st")}
        self.log = []

    def wait(self, dst, srcname, tok):
        if tok <= 0:
            return
        eng = self.e[dst]
        if srcname == dst:
            return
        if eng.waited.get(srcname, 0) >= tok:
            return
        eng.waited[srcname] = tok
        self.log.append(f"{dst}: WAIT {srcname} >= {tok}")
        sem = self.e[srcname].sem
        eng.emit(lambda h, s=sem, t=tok: h.wait_ge(s, t))

    def op(self, engname, fn, reads=(), writes=(), inc=1, sem_eng=None):
        eng = self.e[engname]
        seng = self.e[sem_eng] if sem_eng else eng
        for b in reads:
            for en, t in b.ready.items():
                self.wait(engname, en, t)
        for b in writes:
            for en, t in b.ready.items():
                self.wait(engname, en, t)
            for en, t in b.readers.items():
                self.wait(engname, en, t)
        tok = seng.mile_next(inc)
        sname = seng.name
        self.log.append(f"{engname}: OP {sname}={tok}")
        sem = seng.sem

        def run(h, fn=fn, sem=sem, inc=inc):
            fn(h).then_inc(sem, inc)

        eng.emit(run)
        for b in reads:
            if b.readers.get(sname, 0) < tok:
                b.readers[sname] = tok
        for b in writes:
            if b.ready.get(sname, 0) < tok:
                b.ready[sname] = tok
            b.readers = {}
        return (sname, tok)


DEBUG_TAPS = []
PHASES = "PDFWN"


def build_core(L=8192, NCORES=8):
    nsuper = L // SUPER
    nc = bass.Bass("TRN2", target_bir_lowering=False)
    dbg_slots = {}
    if DEBUG_TAPS:
        dbg_b = nc.dram_tensor("dbg_b", [len(DEBUG_TAPS), P, 9 * SUPER], BF16,
                               kind="ExternalOutput").ap()
        dbg_f = nc.dram_tensor("dbg_f", [len(DEBUG_TAPS), P, 9 * SUPER], F32,
                               kind="ExternalOutput").ap()

    x_hbm = nc.dram_tensor("x", [L, D], F32, kind="ExternalInput").ap()
    wqT = nc.dram_tensor("wqT", [D, D], F32, kind="ExternalInput").ap()
    wkT = nc.dram_tensor("wkT", [D, D], F32, kind="ExternalInput").ap()
    wvT = nc.dram_tensor("wvT", [D, D], F32, kind="ExternalInput").ap()
    woT = nc.dram_tensor("woT", [D, D], F32, kind="ExternalInput").ap()
    gbT = nc.dram_tensor("gbT", [D, 24], F32, kind="ExternalInput").ap()
    gb_bias = nc.dram_tensor("gb_bias", [24, 1], F32, kind="ExternalInput").ap()
    convw = nc.dram_tensor("convw", [D, 12], F32, kind="ExternalInput").ap()
    firw = nc.dram_tensor("firw", [D, NTAP_FIR], F32, kind="ExternalInput").ap()
    win_h = nc.dram_tensor("win", [D, 64], F32, kind="ExternalInput").ap()
    woutE = nc.dram_tensor("woutE", [65, D], F32, kind="ExternalInput").ap()
    smalls = nc.dram_tensor("smalls", [64, 50], F32, kind="ExternalInput").ap()
    nw_h = nc.dram_tensor("nw", [P, 2], F32, kind="ExternalInput").ap()
    hrow = nc.dram_tensor("hrow", [1, 8], F32, kind="ExternalInput").ap()
    y_hbm = nc.dram_tensor("y", [L, D], F32, kind="ExternalOutput").ap()
    NDIAG = 96 + 8 * NTAP_FIR
    diag_dram = nc.dram_tensor("diag_scratch", [NDIAG, P, P], BF16, kind="Internal").ap()

    es = contextlib.ExitStack()
    E = es.enter_context
    with es:
        wq_b = E(nc.sbuf_tensor("wq_b", [P, NKC, D], BF16))
        wk_b = E(nc.sbuf_tensor("wk_b", [P, NKC, D], BF16))
        wv_b = E(nc.sbuf_tensor("wv_b", [P, NKC, D], BF16))
        wo_b = E(nc.sbuf_tensor("wo_b", [P, NKC, D], BF16))
        gb_b = E(nc.sbuf_tensor("gb_b", [P, NKC, 24], BF16))
        gbb_b = E(nc.sbuf_tensor("gbb_b", [24, 1], F32))
        win_b = E(nc.sbuf_tensor("win_b", [P, NKC, 64], BF16))
        woutE_b = E(nc.sbuf_tensor("woutE_b", [P, D], BF16))

        small_f = E(nc.sbuf_tensor("small_f", [64, 50], F32))
        sc_f = E(nc.sbuf_tensor("sc_f", [P, 12], F32))
        masks = E(nc.sbuf_tensor("masks", [P, 4, P], BF16))
        rfin_s = E(nc.sbuf_tensor("rfin_s", [P, H, P], BF16))
        ones_col = E(nc.sbuf_tensor("ones_col", [1, P], F32))
        pidx_f = E(nc.sbuf_tensor("pidx_f", [P, 1], F32))
        S_f = E(nc.sbuf_tensor("S_f", [P, 2, H, DK], F32))
        S_b = E(nc.sbuf_tensor("S_b", [P, 2, H, DK], BF16))
        vwin = E(nc.sbuf_tensor("vwin", [P, NKC, 64 + SUPER], BF16))
        o_tm = E(nc.sbuf_tensor("o_tm", [P, 8, D], BF16))
        prob_t = E(nc.sbuf_tensor("prob_t", [P, 8, 24], F32))
        stat_t = E(nc.sbuf_tensor("stat_t", [P, 8, 24], F32))
        y1n_t = E(nc.sbuf_tensor("y1n_t", [P, 4, D], BF16))
        x_f32 = [E(nc.sbuf_tensor("x_f32_0", [P, D], F32))] * 2
        x_fm = E(nc.sbuf_tensor("x_fm", [P, NKC, SLAB], BF16))
        fm_scr = x_fm
        pre_b = E(nc.sbuf_tensor("pre_b", [P, NKC, 4 + SLAB], BF16))
        x_tmb = pre_b[:].rearrange("p a b -> p (a b)")[:, 0:4 * D].rearrange(
            "p (a b) -> p a b", a=4)
        hist_b = E(nc.sbuf_tensor("hist_b", [P, 3, NKC, 4], BF16))
        qk_sb = E(nc.sbuf_tensor("qk_sb", [P, 2, NKC, SLAB], BF16))
        q_sb = qk_sb[:, 0]
        k_sb = qk_sb[:, 1]
        _qk_f32 = qk_sb[:].rearrange("p a b c -> p (a b c)").bitcast(F32)
        fw_f = _qk_f32[:, 0:NKC * NTAP_FIR].rearrange("p (a b) -> p a b", a=NKC)
        cw_f = _qk_f32[:, 704:704 + NKC * 12].rearrange("p (a b) -> p a b", a=NKC)
        iota_f = _qk_f32[:, 850:850 + P]
        hrow_f = _qk_f32[0:1, 990:1006]
        dstr = [E(nc.sbuf_tensor(f"dstr{i}", [P, 8, P], BF16)) for i in range(2)]
        fb = E(nc.sbuf_tensor("fb", [P, 3, 2, SUPER], BF16))
        gsc_f = E(nc.sbuf_tensor("gsc_f", [P, 4, 8], F32))
        kpos_s2 = [E(nc.sbuf_tensor(f"kpos_s{i}", [P, D], BF16)) for i in range(2)]
        kbn_s2 = [E(nc.sbuf_tensor(f"kbn_s{i}", [P, D], BF16)) for i in range(2)]
        vb_s2 = [E(nc.sbuf_tensor(f"vb_s{i}", [P, D], BF16)) for i in range(2)]
        uadj_s = E(nc.sbuf_tensor("uadj_s", [P, H, DK], BF16))
        a_s = E(nc.sbuf_tensor("a_s", [P, H, P], BF16))
        at_s = E(nc.sbuf_tensor("at_s", [P, H, P], BF16))
        zz_s = E(nc.sbuf_tensor("zz_s", [P, H, P], BF16))
        ww_s = zz_s
        bt_s = E(nc.sbuf_tensor("bt_s", [P, H, P], BF16))
        pq_s = [E(nc.sbuf_tensor(f"pq_s{i}", [P, H, P], BF16)) for i in range(4)]
        rr_s = [E(nc.sbuf_tensor(f"rr_s{i}", [P, H, P], BF16)) for i in range(4)]
        wn_s = E(nc.sbuf_tensor("wn_s", [P, 2, P], BF16))
        acc_f = y1n_t[:].rearrange("p a b -> p (a b)").bitcast(F32).rearrange(
            "p (a b) -> p a b", a=8)
        sq_scr = E(nc.sbuf_tensor("sq_scr", [P, SLAB], F32))
        gl_b = sq_scr[:].bitcast(BF16)[0:24, 0:SLAB]
        rms_f = E(nc.sbuf_tensor("rms_f", [P, 8, H], F32))
        ln_f = E(nc.sbuf_tensor("ln_f", [P, 16], F32))
        _pre_f32 = pre_b[:].rearrange("p a b -> p (a b)").bitcast(F32)
        _pre_b16 = pre_b[:].rearrange("p a b -> p (a b)")
        ffs_f = _pre_f32[0:64, 0:SLAB]
        ffh_f = _pre_f32[0:64, SLAB:2 * SLAB].unsqueeze(1)
        ffg_f = _pre_f32[0:64, 2 * SLAB:3 * SLAB]
        ffg_b = _pre_b16[:, 3 * SLAB * 2:3 * SLAB * 2 + SLAB]
        _otm_flat = o_tm[:].rearrange("p a b -> p (a b)")
        y2_f = [_otm_flat[:, 2048 * i:2048 * (i + 1)].bitcast(F32) for i in range(2)]
        y1_f = y2_f[0]
        tiny_f = E(nc.sbuf_tensor("tiny_f", [P, 16], F32))

        ps_proj = [E(nc.psum_tensor(f"ps_proj{i}", [P, SLAB], F32)) for i in range(2)]
        ps_conv = E(nc.psum_tensor("ps_conv", [P, SLAB], F32))
        ps_fir = E(nc.psum_tensor("ps_fir", [P, SUPER], F32))
        ps_g = E(nc.psum_tensor("ps_g", [P, H, P], F32))
        ps_u = E(nc.psum_tensor("ps_u", [P, SLAB], F32))
        ps_tr = E(nc.psum_tensor("ps_tr", [P, NKC, P], BF16))
        ps_tr2 = ps_g[:].rearrange("p a b -> p (a b)").bitcast(BF16).rearrange(
            "p (a b) -> p a b", a=NKC)
        ps_kv = ps_fir[:].bitcast(BF16).rearrange("p (a b) -> p a b", a=16)
        ps_tr32 = ps_tr[:].rearrange("p a b -> p (a b)").bitcast(F32).rearrange(
            "p (a b) -> p a b", a=H)

        sems = {n: E(nc.semaphore(n)) for n in ("pe", "dve", "act", "pool", "sp",
                                                "ld0", "ld1", "dg0", "dg1", "dg2", "dg3", "st")}
        blk_g = E(nc.Block())

        C = Ctx(nc, sems)
        B = {}

        def buf(name):
            if name not in B:
                B[name] = Buf(name)
            return B[name]

        def dma(qeng, out, in_, reads=(), writes=(), sem_eng=None):
            return C.op(qeng, lambda h, o=out, i=in_: h.dma_start(out=o, in_=i),
                        reads=reads, writes=writes, inc=16, sem_eng=sem_eng)

        def quake_rsqrt(dst, src, eps, scr_x, scr_t, bufs, nr=2):
            rd = list(bufs) + [buf("quake_scr")]
            yi = dst.bitcast(mybir.dt.int32)
            xi = scr_x.bitcast(mybir.dt.int32)
            ti = scr_t.bitcast(mybir.dt.int32)
            C.op("pool", lambda h: h.tensor_scalar(out=scr_x, in0=src, scalar1=float(eps),
                                                   scalar2=None, op0=ALU.add), reads=rd, writes=rd)
            C.op("dve", lambda h: h.tensor_scalar(out=ti, in0=xi, scalar1=1, scalar2=None,
                                                  op0=ALU.arith_shift_right), reads=rd, writes=rd)
            C.op("pool", lambda h: h.tensor_copy(yi, ti), reads=rd, writes=rd)
            C.op("dve", lambda h: h.tensor_scalar(out=yi, in0=yi, scalar1=QUAKE + 1,
                                                  scalar2=-1, op0=ALU.subtract, op1=ALU.mult),
                 reads=rd, writes=rd)
            for _ in range(nr):
                C.op("pool", lambda h: h.tensor_tensor(out=scr_t, in0=dst, in1=dst, op=ALU.mult),
                     reads=rd, writes=rd)
                C.op("dve", lambda h: h.tensor_tensor(out=scr_t, in0=scr_t, in1=scr_x,
                                                      op=ALU.mult), reads=rd, writes=rd)
                C.op("pool", lambda h: h.tensor_scalar(out=scr_t, in0=scr_t, scalar1=-0.5,
                                                      scalar2=1.5, op0=ALU.mult, op1=ALU.add),
                     reads=rd, writes=rd)
                C.op("dve", lambda h: h.tensor_tensor(out=dst, in0=dst, in1=scr_t, op=ALU.mult),
                     reads=rd, writes=rd)

        C.op("pool", lambda h: h.memset(masks[:], 1.0), writes=[buf("masks")])
        mask_defs = ((0, ALU.is_equal, -1, 1),
                     (1, ALU.is_gt, -1, 1),
                     (2, ALU.is_gt, 1, -1),
                     (3, ALU.is_ge, 1, -1))
        for mi, cmp, coef, cm in mask_defs:
            C.op("pool", lambda h, mi=mi, cmp=cmp, coef=coef, cm=cm: h.affine_select(
                out=masks[:, mi, :], in_=masks[:, mi, :], compare_op=cmp, fill=0.0,
                base=0, pattern=[[coef, P]], channel_multiplier=cm), writes=[buf("masks")])
        C.op("pool", lambda h: h.memset(ones_col[:], 1.0), writes=[buf("ones")])
        C.op("pool", lambda h: h.iota(iota_f, pattern=[[1, P]], base=0, channel_multiplier=0,
                                      allow_small_or_imprecise_dtypes=True),
             writes=[buf("iota"), buf("qk_sb")])
        C.op("pool", lambda h: h.iota(pidx_f[:], pattern=[[0, 1]], base=0, channel_multiplier=1,
                                      allow_small_or_imprecise_dtypes=True), writes=[buf("iota")])
        C.op("pool", lambda h: h.memset(S_f[:], 0.0),
             writes=[buf("S_f%d" % _h) for _h in range(H)])
        C.op("pool", lambda h: h.memset(S_b[:], 0.0),
             writes=[buf("S_b%d" % _h) for _h in range(H)])
        C.op("pool", lambda h: h.memset(vwin[:, :, 0:64], 0.0), writes=[buf("vwin")])
        C.op("pool", lambda h: h.memset(hist_b[:], 0.0), writes=[buf("hist")])

        dma("sp", cw_f[:], convw.rearrange("(c p) j -> p c j", p=P), writes=[buf("qk_sb")])
        dma("sp", fw_f[:], firw.rearrange("(c p) j -> p c j", p=P), writes=[buf("qk_sb")])
        dma("sp", small_f[:], smalls, writes=[buf("small")])
        dma("sp", hrow_f[0:1, 0:8], hrow, writes=[buf("hrow"), buf("qk_sb")])
        dma("sp", tiny_f[:, 0:2], nw_h[:, 0:2], writes=[buf("nw")])
        bXS = [buf("xs0"), buf("xs0")]
        dma("sp", x_f32[0][:, 0:192].rearrange("p (c j) -> p c j", c=8),
            gbT.rearrange("(c p) j -> p c j", p=P), writes=[bXS[0]])
        C.op("dve", lambda h: h.tensor_copy(gb_b[:], x_f32[0][:, 0:192].rearrange(
            "p (c j) -> p c j", c=8)), reads=[bXS[0]], writes=[buf("gb_b")])
        dma("sp", gbb_b[:], gb_bias, writes=[buf("gbb")])
        dma("sp", x_f32[0][:, 0:512].rearrange("p (c j) -> p c j", c=8),
            win_h.rearrange("(c p) j -> p c j", p=P), writes=[bXS[0]])
        C.op("dve", lambda h: h.tensor_copy(win_b[:], x_f32[0][:, 0:512].rearrange(
            "p (c j) -> p c j", c=8)), reads=[bXS[0]], writes=[buf("win_b")])
        dma("sp", x_f32[1][0:65, :], woutE, writes=[bXS[1]])
        C.op("dve", lambda h: h.tensor_copy(woutE_b[0:65, :], x_f32[1][0:65, :]),
             reads=[bXS[1]], writes=[buf("woutE")])

        for (w_h, w_t, bname) in ((wqT, wq_b, "wq"), (wkT, wk_b, "wk"), (wvT, wv_b, "wv")):
            for kc in range(NKC):
                rr = w_h.rearrange("(c p) i -> p c i", p=P)[:, kc, :]
                dma("sp", x_f32[kc % 2][:], rr, writes=[bXS[kc % 2]],
                    sem_eng="ld0")
                C.op("dve", lambda h, w_t=w_t, kc=kc: h.tensor_copy(
                    w_t[:, kc, :], x_f32[kc % 2][:]), reads=[bXS[kc % 2]],
                    writes=[buf(bname)])
        for kc in range(NKC):
            rr = woT.rearrange("(c p) i -> p c i", p=P)[:, kc, :]
            dma("sp", x_f32[kc % 2][:], rr, writes=[bXS[kc % 2]],
                sem_eng="ld0")
            C.op("dve", lambda h, kc=kc: h.tensor_scalar(
                out=wo_b[:, kc, :], in0=x_f32[kc % 2][:], scalar1=tiny_f[:, kc % 2:kc % 2 + 1],
                scalar2=None, op0=ALU.mult), reads=[bXS[kc % 2], buf("nw")], writes=[buf("wo")])

        bH = buf("hrow")
        C.op("act", lambda h: h.activation(hrow_f[0:1, 8:12], hrow_f[0:1, 0:4], AF.Exp,
                                           scale=-1.0), reads=[bH], writes=[bH])
        C.op("dve", lambda h: h.tensor_scalar(out=hrow_f[0:1, 12:16], in0=hrow_f[0:1, 4:8],
                                              scalar1=0.0, scalar2=0.2, op0=ALU.max,
                                              op1=ALU.min), reads=[bH], writes=[bH])
        C.op("pe", lambda h: h.matmul(ps_conv[:, 0:4], ones_col[:], hrow_f[0:1, 8:12],
                                      start=True, stop=True), reads=[bH, buf("ones")],
             writes=[buf("ps_conv")])
        C.op("pe", lambda h: h.matmul(ps_conv[:, 8:12], ones_col[:], hrow_f[0:1, 12:16],
                                      start=True, stop=True), reads=[bH], writes=[buf("ps_conv")])
        C.op("dve", lambda h: h.tensor_copy(sc_f[:, 0:4], ps_conv[:, 0:4]),
             reads=[buf("ps_conv")], writes=[buf("sc")])
        C.op("dve", lambda h: h.tensor_copy(sc_f[:, 8:12], ps_conv[:, 8:12]),
             reads=[buf("ps_conv")], writes=[buf("sc")])
        C.op("pool", lambda h: h.tensor_scalar(out=sc_f[:, 4:8], in0=sc_f[:, 8:12],
                                               scalar1=-5.0, scalar2=1.0, op0=ALU.mult,
                                               op1=ALU.add),
             reads=[buf("sc")], writes=[buf("sc")])

        dg_big = o_tm[:].rearrange("p a b -> p (a b)")[:, 0:2 * 8 * P].rearrange(
            "p (s a b) -> p s a b", s=2, a=8)
        bDGB = [buf("dgb0"), buf("dgb1")]
        _coefs = []
        for pj in range(3):
            for kc in range(NKC):
                for j in range(4):
                    _coefs.append((cw_f[:, kc, 4 * pj + j:4 * pj + j + 1],
                                   (pj * 8 + kc) * 4 + j))
        for kc in range(NKC):
            for t in range(NTAP_FIR):
                _coefs.append((fw_f[:, kc, t:t + 1], 96 + kc * NTAP_FIR + t))
        gi = 0
        while gi < len(_coefs):
            n = 1
            while (n < 8 and gi + n < len(_coefs)
                   and _coefs[gi + n][1] == _coefs[gi][1] + n):
                n += 1
            sslot = (gi // 8) % 2
            for k2 in range(n):
                C.op("dve", lambda h, sslot=sslot, k2=k2, c=_coefs[gi + k2][0]:
                     h.tensor_scalar(
                    out=dg_big[:, sslot, k2, :], in0=iota_f, scalar1=pidx_f[:], scalar2=c,
                    op0=ALU.is_equal, op1=ALU.mult), reads=[buf("iota"), buf("qk_sb")],
                    writes=[bDGB[sslot]])
            dma("pool", diag_dram[_coefs[gi][1]:_coefs[gi][1] + n, :, :].rearrange(
                "g p c -> p g c"), dg_big[:, sslot, 0:n, :],
                reads=[bDGB[sslot], buf("o_tm")])
            gi += n

        bXF = [buf("xs0"), buf("xs0")]
        bXT = buf("pre")
        bXFM = buf("x_fm")
        bPRE = buf("pre")
        bHIST = buf("hist")
        bQ = buf("qk_sb")
        bK = buf("qk_sb")
        bVW = buf("vwin")
        bPROB = buf("prob")
        bSTAT = buf("stat")
        bOTM = buf("o_tm")
        bY1N = buf("y1n")
        bFB = buf("qk_sb")
        bACC = buf("y1n")
        bRMS = buf("rms")
        bDS = [buf(f"dstr{i}") for i in range(2)]
        bGL = buf("sq")
        bGSC = buf("gsc")
        bPS_PROJ = [buf("ps_proj0"), buf("ps_proj1")]
        bPS_CONV = buf("ps_conv")
        bPS_FIR = buf("ps_fir")
        bPS_G = buf("ps_g")
        bPS_U0 = buf("ps_u0")
        bPS_U1 = buf("ps_u1")
        bPS_TR = buf("ps_tr")
        bSF4 = [buf("S_f%d" % _h) for _h in range(H)]
        bSB4 = [buf("S_b%d" % _h) for _h in range(H)]
        bKPOS2 = [buf("kpos0"), buf("kpos1")]
        bKBN2 = [buf("kbn0"), buf("kbn1")]
        bVB2 = [buf("vb0"), buf("vb1")]
        bUADJ = buf("uadj")
        bA = buf("a_s")
        bAT = buf("at_s")
        bZZ = buf("zz")
        bWW = buf("zz")
        bBT = buf("bt")
        bPQ = [buf(f"pq{i}") for i in range(4)]
        bRR = [buf(f"rr{i}") for i in range(4)]
        bWN = buf("wn")
        bFM = buf("x_fm")
        bY1F = buf("o_tm")
        bLN = buf("ln")
        bSQ = buf("sq")
        bFFS = buf("pre")
        bFFH = buf("pre")
        bFFG = buf("pre")
        bFFGB = buf("pre")
        bY2 = [buf("o_tm"), buf("o_tm")]
        bSTA = buf("stat_any")

        wbufs = {"wq": buf("wq"), "wk": buf("wk"), "wv": buf("wv"), "wo": buf("wo")}
        _trrot = [0]

        def tr_slot():
            _trrot[0] ^= 1
            return (ps_tr, bPS_TR) if _trrot[0] else (ps_tr2, bPS_G)
        dsi = 0

        def stream_diags(idx0, n):
            nonlocal dsi
            s = dsi % 2
            dsi += 1
            dma("pool", dstr[s][:, 0:n, :],
                diag_dram[idx0:idx0 + n, :, :].rearrange("g p c -> p g c"),
                writes=[bDS[s]], sem_eng=f"dg{s}")
            return s

        def dump(name, ap, rbufs):
            if name not in DEBUG_TAPS:
                return
            i = DEBUG_TAPS.index(name)
            fs = ap.shape[1]
            tgt = dbg_b if ap.dtype == BF16 else dbg_f
            npart = ap.shape[0]
            dma("pool", tgt[i, 0:npart, 0:fs], ap, reads=list(rbufs))

        def phase_P(u, s):
            gslab = u * 2 + s
            t0 = gslab * SLAB
            for blk in range(4):
                tb = t0 + blk * P
                xb = bXF[blk % 2]
                dma("sp", x_f32[blk % 2][:], x_hbm[tb:tb + P, :], writes=[xb],
                    sem_eng="ld0")
                C.op("dve", lambda h, blk=blk: h.tensor_copy(x_tmb[:, blk, :],
                                                             x_f32[blk % 2][:]),
                     reads=[xb], writes=[bXT])
            for blk in range(4):
                pt, bpt = tr_slot()
                for kc in range(NKC):
                    C.op("pe", lambda h, blk=blk, kc=kc, pt=pt: h.transpose(
                        pt[:, kc, :], x_tmb[:, blk, kc * P:kc * P + P], masks[:, 0, :]),
                        reads=[bXT, buf("masks")], writes=[bpt])
                C.op("act", lambda h, blk=blk, pt=pt: h.copy(
                    x_fm[:, :, blk * P:blk * P + P], pt[:]), reads=[bpt], writes=[bXFM])
            for kc in range(NKC):
                C.op("pe", lambda h, kc=kc: h.matmul(
                    ps_conv[0:24, :], gb_b[:, kc, :], x_fm[:, kc, :],
                    start=(kc == 0), stop=(kc == NKC - 1)),
                    reads=[buf("gb_b"), bXFM], writes=[bPS_CONV])
            C.op("dve", lambda h: h.tensor_scalar(out=gl_b[:], in0=ps_conv[0:24, :],
                                                  scalar1=gbb_b[:], scalar2=None,
                                                  op0=ALU.add),
                 reads=[bPS_CONV, buf("gbb")], writes=[bGL])
            for blk in range(4):
                C.op("pe", lambda h, blk=blk: h.transpose(
                    ps_tr[:, blk, 0:24], gl_b[0:24, blk * P:blk * P + P], masks[0:24, 0, 0:24]),
                    reads=[bGL, buf("masks")], writes=[bPS_TR])
            C.op("act", lambda h, s=s: h.copy(prob_t[:, 4 * s:4 * s + 4, :],
                                              ps_tr[:, 0:4, 0:24]),
                 reads=[bPS_TR], writes=[bPROB])
            sb0 = 4 * s
            pv = prob_t[:, sb0:sb0 + 4, :]
            C.op("act", lambda h: h.activation(pv[:, :, 20:24], pv[:, :, 20:24], AF.Sigmoid),
                 reads=[bPROB], writes=[bPROB])
            for hh in range(H):
                C.op("dve", lambda h, hh=hh: h.reduce_max(gsc_f[:, :, hh:hh + 1],
                                                          pv[:, :, 5 * hh:5 * hh + 5], axis=AX.X),
                     reads=[bPROB], writes=[bGSC])
            for blk in range(4):
                for hh in range(H):
                    C.op("dve", lambda h, blk=blk, hh=hh: h.tensor_scalar(
                        out=pv[:, blk, 5 * hh:5 * hh + 5], in0=pv[:, blk, 5 * hh:5 * hh + 5],
                        scalar1=gsc_f[:, blk, hh:hh + 1], scalar2=sc_f[:, hh:hh + 1],
                        op0=ALU.subtract, op1=ALU.mult),
                        reads=[bGSC, buf("sc")], writes=[bPROB])
            C.op("act", lambda h: h.activation(pv[:, :, 0:20], pv[:, :, 0:20], AF.Exp),
                 reads=[bPROB], writes=[bPROB])
            for hh in range(H):
                C.op("dve", lambda h, hh=hh: h.reduce_sum(gsc_f[:, :, 4 + hh:5 + hh],
                                                          pv[:, :, 5 * hh:5 * hh + 5], axis=AX.X),
                     reads=[bPROB], writes=[bGSC])
            C.op("pool", lambda h: h.tensor_copy(gsc_f[:, 0:2, 4:8], gsc_f[:, 0:2, 4:8]),
                 reads=[bGSC], writes=[bGSC])
            C.op("pool", lambda h: h.tensor_copy(gsc_f[:, 2:4, 4:8], gsc_f[:, 2:4, 4:8]),
                 reads=[bGSC], writes=[bGSC])
            C.op("dve", lambda h: h.reciprocal(out=gsc_f[:, :, 4:8], in_=gsc_f[:, :, 4:8]),
                 reads=[bGSC], writes=[bGSC])
            C.op("pool", lambda h: h.tensor_tensor(out=gsc_f[:, :, 4:8], in0=gsc_f[:, :, 4:8],
                                                   in1=sc_f[:, 4:8].unsqueeze(1).broadcast_to(
                                                       [P, 4, 4]), op=ALU.mult),
                 reads=[bGSC, buf("sc")], writes=[bGSC])
            for blk in range(4):
                for hh in range(H):
                    C.op("dve", lambda h, blk=blk, hh=hh: h.scalar_tensor_tensor(
                        out=pv[:, blk, 5 * hh:5 * hh + 5], in0=pv[:, blk, 5 * hh:5 * hh + 5],
                        scalar=gsc_f[:, blk, 4 + hh:5 + hh],
                        in1=sc_f[:, 8 + hh:9 + hh].broadcast_to([P, 5]),
                        op0=ALU.mult, op1=ALU.add),
                        reads=[bGSC, buf("sc")], writes=[bPROB])

            if u == 0 and s == 0:
                dump("prob", prob_t[:].rearrange("p a b -> p (a b)"), [bPROB])
                dump("wq", wq_b[:].rearrange("p a b -> p (a b)"), [wbufs["wq"]])
                dump("x_fm", x_fm[:].rearrange("p a b -> p (a b)"), [bXFM])
                dump("sc", sc_f[:], [buf("sc")])
                dump("hrow", hrow_f[:], [buf("hrow")])
            for pj, (w_t, wb) in enumerate(((wq_b, wbufs["wq"]), (wk_b, wbufs["wk"]),
                                            (wv_b, wbufs["wv"]))):
                C.op("dve", lambda h, pj=pj: h.tensor_copy(pre_b[:, :, 0:4],
                                                           hist_b[:, pj, :, :]),
                     reads=[bHIST], writes=[bPRE])
                for ic in range(NKC):
                    pp = ic % 2
                    for kc in range(NKC):
                        C.op("pe", lambda h, w_t=w_t, ic=ic, kc=kc, pp=pp: h.matmul(
                            ps_proj[pp][:], w_t[:, kc, ic * P:ic * P + P], x_fm[:, kc, :],
                            start=(kc == 0), stop=(kc == NKC - 1)),
                            reads=[wb, bXFM], writes=[bPS_PROJ[pp]])
                    if u == 0 and s == 0 and pj == 0 and ic == 0 and "psq" in DEBUG_TAPS:
                        C.op("act", lambda h: h.copy(sq_scr[:], ps_proj[0][:]),
                             reads=[bPS_PROJ[0]], writes=[bSQ])
                        dump("psq", sq_scr[:], [bSQ])
                    C.op("act", lambda h, ic=ic, pp=pp: h.copy(pre_b[:, ic, 4:4 + SLAB],
                                                               ps_proj[pp][:]),
                         reads=[bPS_PROJ[pp]], writes=[bPRE])
                    if u == 0 and s == 0 and pj == 0 and ic == 0:
                        dump("pre00", pre_b[:, 0, :], [bPRE])
                C.op("dve", lambda h, pj=pj: h.tensor_copy(hist_b[:, pj, :, :],
                                                           pre_b[:, :, SLAB:SLAB + 4]),
                     reads=[bPRE], writes=[bHIST])
                if u == 0 and s == 0 and pj == 0:
                    dump("pre02", pre_b[:, 0, :], [bPRE])
                import os as _os2
                for kc in range(NKC):
                    cps = ps_conv if kc % 2 == 0 else ps_u
                    bcps = [bPS_CONV] if kc % 2 == 0 else [bPS_U0, bPS_U1]
                    if not _os2.environ.get("STREAM_OFF"):
                        sl = stream_diags((pj * 8 + kc) * 4, 4)
                    if u == 0 and s == 0 and pj == 0 and kc == 1:
                        dump("pre03", pre_b[:, 0, :], [bPRE])
                        dump("pre05", pre_b[:, 0, :], [bPRE])
                        dump("pre06", pre_b[:, 0, :], [bPRE])
                    if not _os2.environ.get("CONV_OFF"):
                        for j in range(4):
                            C.op("pe", lambda h, sl=sl, j=j, kc=kc, cps=cps: h.matmul(
                                cps[:], dstr[sl][:, j, :],
                                pre_b[:, kc, j + 1:j + 1 + SLAB],
                                start=(j == 0), stop=(j == 3)),
                                reads=[bDS[sl], bPRE], writes=bcps)
                    if pj == 0:
                        dst, bdst = q_sb[:, kc, :], bQ
                    elif pj == 1:
                        dst, bdst = k_sb[:, kc, :], bK
                    else:
                        dst, bdst = vwin[:, kc, 64 + s * SLAB:64 + s * SLAB + SLAB], bVW
                    import os as _os
                    if _os.environ.get("SILU_SCRATCH"):
                        dst, bdst = o_tm[:, 0, 0:SLAB], buf("o_tm")
                    C.op("act", lambda h, dst=dst, cps=cps: h.activation(dst, cps[:], AF.Silu),
                         reads=bcps, writes=[bdst])
                if u == 0 and s == 0:
                    dump("q_sb" if pj == 0 else ("k_sb" if pj == 1 else "vwin"),
                         (q_sb if pj == 0 else (k_sb if pj == 1 else vwin))[:].rearrange(
                             "p a b -> p (a b)"), [bdst])
                    dump("pre" + str(pj), pre_b[:, 0, :], [bPRE])

        def phase_D_pre(u, b):
            s = b // 4
            i0 = (b % 4) * P
            for kc in range(NKC):
                C.op("pe", lambda h, kc=kc: h.transpose(
                    ps_tr[:, kc, :], q_sb[:, kc, i0:i0 + P], masks[:, 0, :]),
                    reads=[bQ, buf("masks")], writes=[bPS_TR])
            for hh in range(H):
                C.op("act", lambda h, hh=hh: h.activation(
                    sq_scr[:, 0:DK], ps_tr[:, 2 * hh:2 * hh + 2, :].rearrange(
                        "p a b -> p (a b)"), AF.Square,
                    accum_out=stat_t[:, b, hh:hh + 1]), reads=[bPS_TR], writes=[bSTAT, bSQ])
            yield
            for kc in range(NKC):
                C.op("pe", lambda h, kc=kc: h.transpose(
                    ps_tr[:, kc, :], k_sb[:, kc, i0:i0 + P], masks[:, 0, :]),
                    reads=[bK, buf("masks")], writes=[bPS_TR])
            for hh in range(H):
                C.op("act", lambda h, hh=hh: h.activation(
                    sq_scr[:, 0:DK], ps_tr[:, 2 * hh:2 * hh + 2, :].rearrange(
                        "p a b -> p (a b)"), AF.Square,
                    accum_out=stat_t[:, b, 4 + hh:5 + hh]), reads=[bPS_TR], writes=[bSTAT, bSQ])
            yield
            quake_rsqrt(stat_t[:, b, 8:16], stat_t[:, b, 0:8], 1e-6,
                        tiny_f[:, 0:8], tiny_f[:, 8:16], [bSTAT], nr=2)
            yield
            C.op("pool", lambda h: h.tensor_tensor(out=stat_t[:, b, 16:20],
                                                   in0=stat_t[:, b, 12:16],
                                                   in1=prob_t[:, b, 20:24], op=ALU.mult),
                 reads=[bSTAT, bPROB], writes=[bSTAT])
            C.op("dve", lambda h: h.tensor_scalar(out=stat_t[:, b, 20:24],
                                                  in0=stat_t[:, b, 16:20], scalar1=-1.0,
                                                  scalar2=None, op0=ALU.mult),
                 reads=[bSTAT], writes=[bSTAT])
            par = b % 2
            kpos_s, kbn_s, vb_s = kpos_s2[par], kbn_s2[par], vb_s2[par]
            bKPOS, bKBN, bVB = bKPOS2[par], bKBN2[par], bVB2[par]
            for hh in range(H):
                hs = slice(hh * DK, hh * DK + DK)
                pt = ps_tr[:, 2 * hh:2 * hh + 2, :].rearrange("p a b -> p (a b)")
                C.op("dve", lambda h, hh=hh, hs=hs, pt=pt, kpos_s=kpos_s: h.tensor_scalar(
                    out=kpos_s[:, hs], in0=pt,
                    scalar1=stat_t[:, b, 12 + hh:13 + hh], scalar2=None, op0=ALU.mult),
                    reads=[bPS_TR, bSTAT], writes=[bKPOS])
                C.op("dve", lambda h, hh=hh, hs=hs, pt=pt, kbn_s=kbn_s: h.tensor_scalar(
                    out=kbn_s[:, hs], in0=pt,
                    scalar1=stat_t[:, b, 20 + hh:21 + hh], scalar2=None, op0=ALU.mult),
                    reads=[bPS_TR, bSTAT], writes=[bKBN])
            yield
            w0 = 64 + b * P
            for kc in range(NKC):
                C.op("pe", lambda h, kc=kc: h.transpose(
                    ps_tr[:, kc, :], vwin[:, kc, w0:w0 + P], masks[:, 0, :]),
                    reads=[bVW, buf("masks")], writes=[bPS_TR])
            for hh in range(H):
                hs = slice(hh * DK, hh * DK + DK)
                pt = ps_tr[:, 2 * hh:2 * hh + 2, :].rearrange("p a b -> p (a b)")
                C.op("dve", lambda h, hh=hh, hs=hs, pt=pt, vb_s=vb_s: h.tensor_scalar(
                    out=vb_s[:, hs], in0=pt,
                    scalar1=prob_t[:, b, 20 + hh:21 + hh], scalar2=None, op0=ALU.mult),
                    reads=[bPS_TR, bPROB], writes=[bVB])
            yield
            for hh in range(H):
                for j in range(2):
                    C.op("pe", lambda h, hh=hh, j=j: h.matmul(
                        ps_g[:, hh, :], k_sb[:, 2 * hh + j, i0:i0 + P],
                        k_sb[:, 2 * hh + j, i0:i0 + P], start=(j == 0), stop=(j == 1)),
                        reads=[bK], writes=[bPS_G])
            for hh in range(H):
                C.op("dve", lambda h, hh=hh: h.tensor_scalar(
                    out=zz_s[:, hh, :], in0=ps_g[:, hh, :],
                    scalar1=stat_t[:, b, 12 + hh:13 + hh], scalar2=None, op0=ALU.mult),
                    reads=[bPS_G, bSTAT], writes=[bZZ])
            for hh in range(H):
                C.op("pe", lambda h, hh=hh: h.transpose(
                    ps_tr[:, hh, :], zz_s[:, hh, :], masks[:, 0, :]),
                    reads=[bZZ, buf("masks")], writes=[bPS_TR])
            for hh in range(H):
                C.op("dve", lambda h, hh=hh: h.scalar_tensor_tensor(
                    out=a_s[:, hh, :], in0=ps_tr[:, hh, :],
                    scalar=stat_t[:, b, 20 + hh:21 + hh], in1=masks[:, 1, :],
                    op0=ALU.mult, op1=ALU.mult),
                    reads=[bPS_TR, bSTAT, buf("masks")], writes=[bA])
            for hh in range(H):
                C.op("dve", lambda h, hh=hh: h.tensor_scalar(
                    out=ww_s[:, hh, :], in0=ps_g[:, hh, :],
                    scalar1=stat_t[:, b, 20 + hh:21 + hh], scalar2=None, op0=ALU.mult),
                    reads=[bPS_G, bSTAT], writes=[bWW])
            for hh in range(H):
                C.op("pe", lambda h, hh=hh: h.transpose(
                    ps_tr[:, 4 + hh, :], ww_s[:, hh, :], masks[:, 0, :]),
                    reads=[bWW, buf("masks")], writes=[bPS_TR])
            for hh in range(H):
                C.op("dve", lambda h, hh=hh: h.scalar_tensor_tensor(
                    out=at_s[:, hh, :], in0=ps_tr[:, 4 + hh, :],
                    scalar=stat_t[:, b, 12 + hh:13 + hh], in1=masks[:, 2, :],
                    op0=ALU.mult, op1=ALU.mult),
                    reads=[bPS_TR, bSTAT, buf("masks")], writes=[bAT])
            yield
            PC, PN, QC, QN = 0, 1, 2, 3
            RC, RN, RTC, RTN = 0, 1, 2, 3
            C.op("dve", lambda h, pc=PC: h.tensor_copy(pq_s[pc][:], at_s[:]), reads=[bAT],
                 writes=[bPQ[PC]])
            C.op("dve", lambda h, qc=QC: h.tensor_copy(pq_s[qc][:], a_s[:]), reads=[bA],
                 writes=[bPQ[QC]])
            for hh in range(H):
                C.op("dve", lambda h, hh=hh, rc=RC, pc=PC: h.tensor_tensor(
                    out=rr_s[rc][:, hh, :], in0=pq_s[pc][:, hh, :], in1=masks[:, 0, :],
                    op=ALU.add), reads=[bPQ[PC], buf("masks")], writes=[bRR[RC]])
                C.op("dve", lambda h, hh=hh, rtc=RTC, qc=QC: h.tensor_tensor(
                    out=rr_s[rtc][:, hh, :], in0=pq_s[qc][:, hh, :], in1=masks[:, 0, :],
                    op=ALU.add), reads=[bPQ[QC], buf("masks")], writes=[bRR[RTC]])
            yield
            for kstep in range(1, 7):
                yield
                for hh in range(H):
                    C.op("pe", lambda h, hh=hh, qc=QC, pc=PC: h.matmul(
                        ps_g[:, hh, :], pq_s[qc][:, hh, :], pq_s[pc][:, hh, :],
                        start=True, stop=True), reads=[bPQ[QC], bPQ[PC]], writes=[bPS_G])
                C.op("dve", lambda h, pn=PN: h.tensor_copy(pq_s[pn][:], ps_g[:]),
                     reads=[bPS_G], writes=[bPQ[PN]])
                if kstep <= 5:
                    for hh in range(H):
                        C.op("pe", lambda h, hh=hh, pc=PC, qc=QC: h.matmul(
                            ps_tr32[:, hh, :], pq_s[pc][:, hh, :], pq_s[qc][:, hh, :],
                            start=True, stop=True), reads=[bPQ[PC], bPQ[QC]], writes=[bPS_TR])
                    C.op("act", lambda h, qn=QN: h.copy(pq_s[qn][:], ps_tr32[:]),
                         reads=[bPS_TR], writes=[bPQ[QN]])
                for hh in range(H):
                    C.op("pe", lambda h, hh=hh, rtc=RTC, pn=PN: h.matmul(
                        ps_g[:, hh, :], rr_s[rtc][:, hh, :], pq_s[pn][:, hh, :],
                        start=True, stop=True), reads=[bRR[RTC], bPQ[PN]], writes=[bPS_G])
                C.op("dve", lambda h, rn=RN, rc=RC: h.tensor_tensor(
                    out=rr_s[rn][:], in0=ps_g[:], in1=rr_s[rc][:], op=ALU.add),
                     reads=[bPS_G, bRR[RC]], writes=[bRR[RN]])
                if kstep <= 5:
                    for hh in range(H):
                        C.op("pe", lambda h, hh=hh, pn=PN, rtc=RTC: h.matmul(
                            ps_tr32[:, hh, :], pq_s[pn][:, hh, :], rr_s[rtc][:, hh, :],
                            start=True, stop=True), reads=[bPQ[PN], bRR[RTC]], writes=[bPS_TR])
                    C.op("act", lambda h, rtn=RTN: h.copy(rr_s[rtn][:], ps_tr32[:]),
                         reads=[bPS_TR], writes=[bRR[RTN]])
                    C.op("dve", lambda h, rtn=RTN, rtc=RTC: h.tensor_tensor(
                        out=rr_s[rtn][:], in0=rr_s[rtn][:], in1=rr_s[rtc][:], op=ALU.add),
                         reads=[bRR[RTN], bRR[RTC]], writes=[bRR[RTN]])
                if u == 0 and b == 0:
                    dump(f"p{kstep}", pq_s[PN][:].rearrange("p a b -> p (a b)"), [bPQ[PN]])
                    dump(f"r{kstep}", rr_s[RN][:].rearrange("p a b -> p (a b)"), [bRR[RN]])
                PC, PN = PN, PC
                QC, QN = QN, QC
                RC, RN = RN, RC
                RTC, RTN = RTN, RTC
            phase_D_pre.rfin_rc = RC

        def phase_D_tail(u, b):
            i0 = (b % 4) * P
            for hh in range(H):
                for j in range(2):
                    C.op("pe", lambda h, hh=hh, j=j: h.matmul(
                        ps_g[:, hh, :], k_sb[:, 2 * hh + j, i0:i0 + P],
                        q_sb[:, 2 * hh + j, i0:i0 + P], start=(j == 0), stop=(j == 1)),
                        reads=[bK, bQ], writes=[bPS_G])
            for hh in range(H):
                C.op("dve", lambda h, hh=hh: h.scalar_tensor_tensor(
                    out=bt_s[:, hh, :], in0=ps_g[:, hh, :],
                    scalar=stat_t[:, b, 12 + hh:13 + hh], in1=masks[:, 3, :],
                    op0=ALU.mult, op1=ALU.mult),
                    reads=[bPS_G, bSTAT, buf("masks")], writes=[bBT])
            rc = phase_D_pre.rfin_rc
            C.op("act", lambda h, rc=rc: h.copy(rfin_s[:], rr_s[rc][:]),
                 reads=[bRR[rc]], writes=[buf("rfin")])
            if u == 0 and b == 0:
                dump("rfin", rfin_s[:].rearrange("p a b -> p (a b)"), [buf("rfin")])

        def phase_D_scan(u, b):
            s = b // 4
            i0 = (b % 4) * P
            par = b % 2
            kpos_s, kbn_s, vb_s = kpos_s2[par], kbn_s2[par], vb_s2[par]
            bKPOS, bKBN, bVB = bKPOS2[par], bKBN2[par], bVB2[par]
            for hh in range(H):
                yield
                hs = slice(hh * DK, hh * DK + DK)
                for j in range(2):
                    C.op("pe", lambda h, hh=hh, j=j: h.matmul(
                        ps_conv[:, j * P:j * P + P],
                        kbn_s[:, hh * DK + j * P:hh * DK + j * P + P],
                        rfin_s[:, hh, :], start=True, stop=True),
                        reads=[bKBN, buf("rfin")], writes=[bPS_CONV])
                C.op("act", lambda h: h.copy(
                    wn_s[:].rearrange("p a b -> p (a b)"), ps_conv[:, 0:2 * P]),
                     reads=[bPS_CONV], writes=[bWN])
                yield
                C.op("pe", lambda h, hh=hh, hs=hs: h.matmul(
                    ps_u[:, 0:DK], rfin_s[:, hh, :], vb_s[:, hs], start=True, stop=False),
                    reads=[buf("rfin"), bVB], writes=[bPS_U0])
                for j in range(2):
                    C.op("pe", lambda h, hh=hh, j=j: h.matmul(
                        ps_u[:, 0:DK], wn_s[:, j, :], S_b[:, j, hh, :],
                        start=False, stop=(j == 1)), reads=[bWN, bSB4[hh]], writes=[bPS_U0])
                C.op("act", lambda h, hh=hh: h.copy(uadj_s[:, hh, :], ps_u[:, 0:DK]),
                     reads=[bPS_U0], writes=[bUADJ])
                yield
                for j in range(2):
                    C.op("pe", lambda h, hh=hh, j=j: h.matmul(
                        ps_fir[:, 0:DK], q_sb[:, 2 * hh + j, i0:i0 + P], S_b[:, j, hh, :],
                        start=(j == 0), stop=False), reads=[bQ, bSB4[hh]], writes=[bPS_FIR])
                C.op("pe", lambda h, hh=hh: h.matmul(
                    ps_fir[:, 0:DK], bt_s[:, hh, :], uadj_s[:, hh, :],
                    start=False, stop=True), reads=[bBT, bUADJ], writes=[bPS_FIR])
                C.op("dve", lambda h, hh=hh, hs=hs: h.tensor_scalar(
                    out=o_tm[:, b, hs], in0=ps_fir[:, 0:DK],
                    scalar1=stat_t[:, b, 8 + hh:9 + hh], scalar2=None, op0=ALU.mult),
                    reads=[bPS_FIR, bSTAT], writes=[bOTM])
                yield
                for j in range(2):
                    C.op("pe", lambda h, hh=hh, j=j: h.matmul(
                        ps_proj[0][:, j * DK:j * DK + DK],
                        kpos_s[:, hh * DK + j * P:hh * DK + j * P + P], uadj_s[:, hh, :],
                        start=True, stop=True), reads=[bKPOS, bUADJ], writes=[bPS_PROJ[0]])
                for j in range(2):
                    C.op("dve", lambda h, hh=hh, j=j: h.tensor_tensor(
                        out=S_f[:, j, hh, :], in0=S_f[:, j, hh, :],
                        in1=ps_proj[0][:, j * DK:j * DK + DK],
                        op=ALU.add), reads=[bPS_PROJ[0], bSF4[hh]], writes=[bSF4[hh]])
                for j in range(2):
                    C.op("act", lambda h, hh=hh, j=j: h.copy(S_b[:, j, hh, :],
                                                             S_f[:, j, hh, :]),
                         reads=[bSF4[hh]], writes=[bSB4[hh]])

            if u == 0 and b == 1:
                dump("uadj1", uadj_s[:].rearrange("p a b -> p (a b)"), [bUADJ])
                dump("od1", o_tm[:, 1, :], [bOTM])
                dump("S2", S_f[:].rearrange("p a b c -> p (a b c)"), list(bSF4))
            if u == 0 and b == 0:
                dump("stat", stat_t[:].rearrange("p a b -> p (a b)"), [bSTAT])
                dump("a_s", a_s[:].rearrange("p a b -> p (a b)"), [bA])
                dump("at_s", at_s[:].rearrange("p a b -> p (a b)"), [bAT])
                dump("bt_s", bt_s[:].rearrange("p a b -> p (a b)"), [bBT])
                dump("uadj", uadj_s[:].rearrange("p a b -> p (a b)"), [bUADJ])
                dump("kpos", kpos_s[:], [bKPOS])
                dump("vb", vb_s[:], [bVB])
                dump("o_d", o_tm[:, 0, :], [bOTM])
                dump("S1", S_f[:].rearrange("p a b c -> p (a b c)"), list(bSF4))

        def phase_FC(u, hh):
            for ci, kc in enumerate((2 * hh, 2 * hh + 1)):
                for br in range(3):
                    K = FIR_K[br]
                    base = 96 + kc * NTAP_FIR + FIR_OFF[br]
                    tdone = 0
                    while tdone < K:
                        g = min(8, K - tdone)
                        sl = stream_diags(base + tdone, g)
                        for jj in range(g):
                            j = tdone + jj
                            c0 = 64 + j - (K - 1)
                            for half in range(2):
                                C.op("pe", lambda h, sl=sl, jj=jj, kc=kc, c0=c0, half=half,
                                     j=j, K=K: h.matmul(
                                         ps_fir[:, half * SLAB:half * SLAB + SLAB],
                                         dstr[sl][:, jj, :],
                                         vwin[:, kc, c0 + half * SLAB:c0 + half * SLAB + SLAB],
                                         start=(j == 0), stop=(j == K - 1)),
                                     reads=[bDS[sl], bVW], writes=[bPS_FIR])
                        tdone += g
                    C.op("dve", lambda h, br=br, ci=ci: h.tensor_copy(
                        fb[:, br, ci, :], ps_fir[:]), reads=[bPS_FIR], writes=[bFB])
            for blk in range(8):
                pt, bpt = tr_slot()
                for br in range(3):
                    for ci in range(2):
                        C.op("pe", lambda h, br=br, ci=ci, blk=blk, pt=pt: h.transpose(
                            pt[:, br * 2 + ci, :], fb[:, br, ci, blk * P:blk * P + P],
                            masks[:, 0, :]), reads=[bFB, buf("masks")], writes=[bpt])
                for ci in range(2):
                    C.op("pe", lambda h, ci=ci, blk=blk, hh=hh, pt=pt: h.transpose(
                        pt[:, 6 + ci, :], vwin[:, 2 * hh + ci, 64 + blk * P:64 + blk * P + P],
                        masks[:, 0, :]), reads=[bVW, buf("masks")], writes=[bpt])
                pvb = prob_t[:, blk, :]
                C.op("dve", lambda h, blk=blk, hh=hh, pvb=pvb, pt=pt: h.tensor_scalar(
                    out=acc_f[:, blk, :], in0=pt[:, 0:2, :].rearrange("p a b -> p (a b)"),
                    scalar1=pvb[:, 5 * hh + 0:5 * hh + 1], scalar2=None, op0=ALU.mult),
                    reads=[bpt, bPROB], writes=[bACC])
                for br in (1, 2):
                    C.op("dve", lambda h, blk=blk, hh=hh, br=br, pvb=pvb, pt=pt:
                         h.scalar_tensor_tensor(
                        out=acc_f[:, blk, :],
                        in0=pt[:, 2 * br:2 * br + 2, :].rearrange("p a b -> p (a b)"),
                        scalar=pvb[:, 5 * hh + br:5 * hh + br + 1],
                        in1=acc_f[:, blk, :], op0=ALU.mult, op1=ALU.add),
                        reads=[bpt, bPROB, bACC], writes=[bACC])
                C.op("dve", lambda h, blk=blk, hh=hh, pvb=pvb: h.scalar_tensor_tensor(
                    out=acc_f[:, blk, :], in0=o_tm[:, blk, hh * DK:hh * DK + DK],
                    scalar=pvb[:, 5 * hh + 3:5 * hh + 4], in1=acc_f[:, blk, :],
                    op0=ALU.mult, op1=ALU.add), reads=[bOTM, bPROB, bACC], writes=[bACC])
                C.op("dve", lambda h, blk=blk, hh=hh, pvb=pvb, pt=pt: h.scalar_tensor_tensor(
                    out=acc_f[:, blk, :], in0=pt[:, 6:8, :].rearrange("p a b -> p (a b)"),
                    scalar=pvb[:, 5 * hh + 4:5 * hh + 5], in1=acc_f[:, blk, :],
                    op0=ALU.mult, op1=ALU.add), reads=[bpt, bPROB, bACC], writes=[bACC])
                C.op("act", lambda h, blk=blk, hh=hh: h.activation(
                    sq_scr[:, 0:DK], acc_f[:, blk, :], AF.Square,
                    accum_out=rms_f[:, blk, hh:hh + 1]), reads=[bACC], writes=[bRMS, bSQ])
            C.op("dve", lambda h, hh=hh: h.tensor_scalar(
                out=tiny_f[:, 0:8], in0=rms_f[:, :, hh], scalar1=1.0 / DK, scalar2=None,
                op0=ALU.mult), reads=[bRMS, buf("quake_scr")], writes=[buf("quake_scr")])
            quake_rsqrt(rms_f[:, :, hh], tiny_f[:, 0:8], 1e-5,
                        tiny_f[:, 8:16], tiny_f[:, 0:8].bitcast(F32), [bRMS], nr=2)
            for blk in range(8):
                C.op("dve", lambda h, blk=blk, hh=hh: h.tensor_scalar(
                    out=o_tm[:, blk, hh * DK:hh * DK + DK], in0=acc_f[:, blk, :],
                    scalar1=rms_f[:, blk, hh:hh + 1], scalar2=None, op0=ALU.mult),
                    reads=[bACC, bRMS], writes=[bOTM])

            if u == 0 and hh == 0:
                dump("fb", fb[:].rearrange("p a b c -> p (a b c)"), [bFB])
                dump("o_c", o_tm[:, 0, :], [bOTM])
                dump("rms", rms_f[:].rearrange("p a b -> p (a b)"), [bRMS])

        def phase_W(u, s):
            gslab = u * 2 + s
            t0 = gslab * SLAB
            for blk in range(4):
                gb = s * 4 + blk
                pt, bpt = tr_slot()
                for kc in range(NKC):
                    C.op("pe", lambda h, gb=gb, kc=kc, pt=pt: h.transpose(
                        pt[:, kc, :], o_tm[:, gb, kc * P:kc * P + P], masks[:, 0, :]),
                        reads=[bOTM, buf("masks")], writes=[bpt])
                C.op("dve", lambda h, blk=blk, pt=pt: h.tensor_copy(
                    fm_scr[:, :, blk * P:blk * P + P], pt[:]), reads=[bpt], writes=[bFM])
            for blk in range(4):
                tb = t0 + blk * P
                xb = bXF[blk % 2]
                dma("sp", x_f32[blk % 2][:], x_hbm[tb:tb + P, :], writes=[xb],
                    sem_eng="ld0")
                for ih in range(2):
                    for kc in range(NKC):
                        C.op("pe", lambda h, blk=blk, ih=ih, kc=kc: h.matmul(
                            ps_proj[ih][:], fm_scr[:, kc, blk * P:blk * P + P],
                            wo_b[:, kc, ih * SLAB:ih * SLAB + SLAB],
                            start=(kc == 0), stop=(kc == NKC - 1)),
                            reads=[bFM, wbufs["wo"]], writes=[bPS_PROJ[ih]])
                for ih in range(2):
                    C.op("dve", lambda h, blk=blk, ih=ih: h.tensor_tensor(
                        out=y1_f[:, ih * SLAB:ih * SLAB + SLAB], in0=ps_proj[ih][:],
                        in1=x_f32[blk % 2][:, ih * SLAB:ih * SLAB + SLAB], op=ALU.add),
                        reads=[bPS_PROJ[ih], xb], writes=[bY1F])
                C.op("dve", lambda h: h.reduce_sum(ln_f[:, 0:1], y1_f[:], axis=AX.X),
                     reads=[bY1F], writes=[bLN, bSTA])
                for ih in range(2):
                    C.op("act", lambda h, ih=ih: h.activation(
                        sq_scr[:], y1_f[:, ih * SLAB:ih * SLAB + SLAB], AF.Square,
                        accum_out=ln_f[:, 1 + 2 * ih:2 + 2 * ih]),
                        reads=[bY1F], writes=[bLN, bSQ, bSTA])
                C.op("dve", lambda h: h.tensor_tensor(out=ln_f[:, 1:2], in0=ln_f[:, 1:2],
                                                      in1=ln_f[:, 3:4], op=ALU.add),
                     reads=[bLN], writes=[bLN])
                C.op("pool", lambda h: h.tensor_scalar(out=ln_f[:, 0:1], in0=ln_f[:, 0:1],
                                                       scalar1=1.0 / D, scalar2=None,
                                                       op0=ALU.mult), reads=[bLN], writes=[bLN])
                C.op("dve", lambda h: h.tensor_tensor(out=ln_f[:, 2:3], in0=ln_f[:, 0:1],
                                                      in1=ln_f[:, 0:1], op=ALU.mult),
                     reads=[bLN], writes=[bLN])
                C.op("pool", lambda h: h.tensor_scalar(out=ln_f[:, 1:2], in0=ln_f[:, 1:2],
                                                       scalar1=1.0 / D, scalar2=None,
                                                       op0=ALU.mult), reads=[bLN], writes=[bLN])
                C.op("dve", lambda h: h.tensor_tensor(out=ln_f[:, 9:10], in0=ln_f[:, 1:2],
                                                      in1=ln_f[:, 2:3], op=ALU.subtract),
                     reads=[bLN], writes=[bLN])
                quake_rsqrt(ln_f[:, 1:2], ln_f[:, 9:10], 1e-5,
                            tiny_f[:, 0:1], tiny_f[:, 1:2], [bLN], nr=2)
                C.op("dve", lambda h, blk=blk: h.tensor_scalar(
                    out=y1n_t[:, blk, :], in0=y1_f[:], scalar1=ln_f[:, 0:1],
                    scalar2=ln_f[:, 1:2], op0=ALU.subtract, op1=ALU.mult),
                    reads=[bY1F, bLN], writes=[bY1N])

            if u == 0 and s == 0:
                dump("y1n", y1n_t[:].rearrange("p a b -> p (a b)"), [bY1N])

        def phase_N(u, s):
            gslab = u * 2 + s
            t0 = gslab * SLAB
            for blk in range(4):
                pt, bpt = tr_slot()
                for kc in range(NKC):
                    C.op("pe", lambda h, blk=blk, kc=kc, pt=pt: h.transpose(
                        pt[:, kc, :], y1n_t[:, blk, kc * P:kc * P + P], masks[:, 0, :]),
                        reads=[bY1N, buf("masks")], writes=[bpt])
                C.op("dve", lambda h, blk=blk, pt=pt: h.tensor_copy(
                    fm_scr[:, :, blk * P:blk * P + P], pt[:]), reads=[bpt], writes=[bFM])
            for kc in range(NKC):
                C.op("pe", lambda h, kc=kc: h.matmul(
                    ps_proj[0][0:64, :], win_b[:, kc, :], fm_scr[:, kc, :],
                    start=(kc == 0), stop=(kc == NKC - 1)),
                    reads=[buf("win_b"), bFM], writes=[bPS_PROJ[0]])
            C.op("dve", lambda h: h.tensor_scalar(out=ffs_f[:], in0=ps_proj[0][0:64, :],
                                                  scalar1=small_f[:, 49:50], scalar2=None,
                                                  op0=ALU.add), reads=[bPS_PROJ[0], buf("small")],
                 writes=[bFFS])
            for i in range(16):
                C.op("act", lambda h, i=i: h.activation(
                    ffh_f[:, 0, :], ffs_f[:], AF.Relu, bias=small_f[:, 16 + i:17 + i],
                    scale=small_f[:, i:i + 1]), reads=[bFFS, buf("small")], writes=[bFFH])
                if i == 0:
                    C.op("dve", lambda h, i=i: h.tensor_scalar(
                        out=ffg_f[:], in0=ffh_f[:, 0, :], scalar1=small_f[:, 32 + i:33 + i],
                        scalar2=small_f[:, 48:49], op0=ALU.mult, op1=ALU.add),
                        reads=[bFFH, buf("small")], writes=[bFFG])
                else:
                    C.op("dve", lambda h, i=i: h.scalar_tensor_tensor(
                        out=ffg_f[:], in0=ffh_f[:, 0, :], scalar=small_f[:, 32 + i:33 + i],
                        in1=ffg_f[:], op0=ALU.mult, op1=ALU.add),
                        reads=[bFFH, buf("small"), bFFG], writes=[bFFG])
            C.op("dve", lambda h: h.tensor_copy(ffg_b[0:64, :], ffg_f[:]),
                 reads=[bFFG], writes=[bFFGB])
            C.op("dve", lambda h: h.memset(ffg_b[64:65, :], 1.0), writes=[bFFGB])
            for blk in range(4):
                tb = t0 + blk * P
                yb = bY2[blk % 2]
                for ih in range(2):
                    C.op("pe", lambda h, blk=blk, ih=ih: h.matmul(
                        ps_proj[ih][:], ffg_b[0:65, blk * P:blk * P + P],
                        woutE_b[0:65, ih * SLAB:ih * SLAB + SLAB], start=True, stop=True),
                        reads=[bFFGB, buf("woutE")], writes=[bPS_PROJ[ih]])
                    C.op("dve", lambda h, blk=blk, ih=ih: h.tensor_tensor(
                        out=y2_f[blk % 2][:, ih * SLAB:ih * SLAB + SLAB], in0=ps_proj[ih][:],
                        in1=y1n_t[:, blk, ih * SLAB:ih * SLAB + SLAB], op=ALU.add),
                        reads=[bPS_PROJ[ih], bY1N], writes=[yb])
                C.op("dve", lambda h, blk=blk: h.reduce_sum(ln_f[:, 4:5], y2_f[blk % 2][:],
                                                            axis=AX.X), reads=[yb],
                     writes=[bLN, bSTA])
                for ih in range(2):
                    C.op("act", lambda h, blk=blk, ih=ih: h.activation(
                        sq_scr[:], y2_f[blk % 2][:, ih * SLAB:ih * SLAB + SLAB], AF.Square,
                        accum_out=ln_f[:, 5 + 2 * ih:6 + 2 * ih]),
                        reads=[yb], writes=[bLN, bSQ, bSTA])
                C.op("dve", lambda h: h.tensor_tensor(out=ln_f[:, 5:6], in0=ln_f[:, 5:6],
                                                      in1=ln_f[:, 7:8], op=ALU.add),
                     reads=[bLN], writes=[bLN])
                C.op("pool", lambda h: h.tensor_scalar(out=ln_f[:, 4:5], in0=ln_f[:, 4:5],
                                                       scalar1=1.0 / D, scalar2=None,
                                                       op0=ALU.mult), reads=[bLN], writes=[bLN])
                C.op("dve", lambda h: h.tensor_tensor(out=ln_f[:, 8:9], in0=ln_f[:, 4:5],
                                                      in1=ln_f[:, 4:5], op=ALU.mult),
                     reads=[bLN], writes=[bLN])
                C.op("pool", lambda h: h.tensor_scalar(out=ln_f[:, 5:6], in0=ln_f[:, 5:6],
                                                       scalar1=1.0 / D, scalar2=None,
                                                       op0=ALU.mult), reads=[bLN], writes=[bLN])
                C.op("dve", lambda h: h.tensor_tensor(out=ln_f[:, 10:11], in0=ln_f[:, 5:6],
                                                      in1=ln_f[:, 8:9], op=ALU.subtract),
                     reads=[bLN], writes=[bLN])
                quake_rsqrt(ln_f[:, 5:6], ln_f[:, 10:11], 1e-5,
                            tiny_f[:, 2:3], tiny_f[:, 3:4], [bLN], nr=2)
                C.op("dve", lambda h, blk=blk: h.tensor_scalar(
                    out=y2_f[blk % 2][:], in0=y2_f[blk % 2][:], scalar1=ln_f[:, 4:5],
                    scalar2=ln_f[:, 5:6], op0=ALU.subtract, op1=ALU.mult),
                    reads=[yb, bLN], writes=[yb])
                nonlocal_store(tb, blk, yb)

        store_count = [0]

        def nonlocal_store(tb, blk, yb):
            dma("pool", y_hbm[tb:tb + P, :], y2_f[blk % 2][:], reads=[yb], sem_eng="st")
            store_count[0] += 1

        for u in range(nsuper):
            for s in range(2):
                if "P" in PHASES:
                    phase_P(u, s)
                if "D" in PHASES:
                    def drain(*gens):
                        gens = [g for g in gens if g is not None]
                        while gens:
                            for g in list(gens):
                                try:
                                    next(g)
                                except StopIteration:
                                    gens.remove(g)
                    import os as _os3
                    b0 = 4 * s
                    if _os3.environ.get("NO_IL"):
                        for b in range(b0, b0 + 4):
                            drain(phase_D_pre(u, b))
                            phase_D_tail(u, b)
                            drain(phase_D_scan(u, b))
                    else:
                        drain(phase_D_pre(u, b0))
                        phase_D_tail(u, b0)
                        for b in range(b0 + 1, b0 + 4):
                            drain(phase_D_scan(u, b - 1), phase_D_pre(u, b))
                            phase_D_tail(u, b)
                        drain(phase_D_scan(u, b0 + 3))
            if "F" in PHASES:
                for hh in range(H):
                    phase_FC(u, hh)
            for s in range(2):
                if "W" in PHASES:
                    phase_W(u, s)
                if "N" in PHASES:
                    phase_N(u, s)
            C.op("dve", lambda h: h.tensor_copy(vwin[:, :, 0:64], vwin[:, :, SUPER:SUPER + 64]),
                 reads=[bVW], writes=[bVW])

        total_pool = C.e["pool"].n
        total_st = C.e["st"].n
        C.e["pool"].emit(lambda h, t=total_pool: h.wait_ge(sems["pool"], t))
        C.e["pool"].emit(lambda h, t=total_st: h.wait_ge(sems["st"], t))
        for qn in ("dg0", "dg1", "dg2", "dg3", "ld0", "ld1"):
            tq = C.e[qn].n
            if tq:
                C.e["pool"].emit(lambda h, t=tq, s=sems[qn]: h.wait_ge(s, t))

        def replay(name):
            def f(h):
                for fn in C.e[name].script:
                    fn(h)
            return f

        blk_g.sync(replay("sp"))
        blk_g.tensor(replay("pe"))
        blk_g.vector(replay("dve"))
        blk_g.scalar(replay("act"))
        blk_g.gpsimd(replay("pool"))

    return nc


def _host_inputs(inputs, L):
    a = inputs
    wq = np.ascontiguousarray(a["L0_attn_Wq"].T)
    wk = np.ascontiguousarray(a["L0_attn_Wk"].T)
    wv = np.ascontiguousarray(a["L0_attn_Wv"].T)
    wo = np.ascontiguousarray(a["L0_attn_Wo"].T)
    gb = np.concatenate([a["L0_attn_gate_w"].reshape(20, D), a["L0_attn_Wb"]], 0)
    gbT = np.ascontiguousarray(gb.T)
    gb_bias = np.concatenate([a["L0_attn_gate_b"].reshape(20), np.zeros(4, np.float32)])
    gb_bias = np.ascontiguousarray(gb_bias.reshape(24, 1).astype(np.float32))
    convw = np.ascontiguousarray(np.concatenate(
        [a["L0_attn_conv_q"], a["L0_attn_conv_k"], a["L0_attn_conv_v"]], 1))
    firw = np.ascontiguousarray(np.concatenate(
        [a["L0_attn_fir_s"].reshape(D, 3), a["L0_attn_fir_m"].reshape(D, 15),
         a["L0_attn_fir_l"].reshape(D, 63)], 1))
    win = np.ascontiguousarray(a["L0_ffn_Win"])
    woutE = np.ascontiguousarray(np.concatenate(
        [a["L0_ffn_Wout"], a["L0_ffn_bout"].reshape(1, D)], 0))
    smalls = np.zeros((64, 50), np.float32)
    smalls[:, 0:16] = a["L0_ffn_W1"]
    smalls[:, 16:32] = a["L0_ffn_b1"]
    smalls[:, 32:48] = a["L0_ffn_W2"]
    smalls[:, 48] = a["L0_ffn_b2"]
    smalls[:, 49] = a["L0_ffn_bin_"]
    nw = np.ascontiguousarray(a["L0_attn_norm_w"].reshape(2, P).T)
    hrow = np.zeros((1, 8), np.float32)
    hrow[0, 0:4] = a["L0_attn_log_temp"].reshape(4)
    hrow[0, 4:8] = a["L0_attn_eps_param"].reshape(4)
    common = dict(wqT=wq, wkT=wk, wvT=wv, woT=wo, gbT=gbT, gb_bias=gb_bias, convw=convw,
                  firw=firw, win=win, woutE=woutE, smalls=smalls, nw=nw, hrow=hrow)
    x = a["L0_attn_hidden_states"]
    maps = []
    for c in range(x.shape[0]):
        m = dict(common)
        m["x"] = np.ascontiguousarray(x[c, :L, :])
        maps.append(m)
    return maps


def kernel(**inputs):
    L = inputs["L0_attn_hidden_states"].shape[1]
    ncores = inputs["L0_attn_hidden_states"].shape[0]
    nc = build_core(L=L, NCORES=ncores)
    maps = _host_inputs(inputs, L)
    res = run_bass_kernel_spmd(nc, maps, core_ids=list(range(ncores)))
    out = np.stack([r["y"] for r in res.results], 0)
    return out.astype(np.float32)
